# version 07
# baseline (speedup 1.0000x reference)
import numpy as np
import ml_dtypes
import concourse.bass as bass
import concourse.mybir as mybir
from contextlib import ExitStack

F32 = mybir.dt.float32
BF16 = mybir.dt.bfloat16
I32 = mybir.dt.int32
AF = mybir.ActivationFunctionType
ALU = mybir.AluOpType
BF = ml_dtypes.bfloat16
PI = float(np.pi)
TWO_PI_HI = float(np.float32(2 * np.pi))
TWO_PI_LO = float(2 * np.pi - np.float64(np.float32(2 * np.pi)))
LN1E6 = float(np.log(1e6))
SCALE = float(128 ** -0.5)

D = 2048
NCH = 16
T = 4096
TB = 512
NB = T // TB
NOB = 4
NH, NKV = 16, 4
I_DIM = 8192
NIC = 64
Q0S = [[0, 512, 3072, 3584], [1024, 1536, 2048, 2560]]
P1_BLOCKS = NB + NOB
P1_SUB = 9
P1_FINE = 9
P0_CLOSE = True
P3_ON = True
NQB = NOB
DELAY = 1000
DELAY_S = 200

ENGS = ("pe", "ve", "sc", "sp", "gp")
DEFSEM = {"pe": "pC", "ve": "vC", "sc": "sC", "gp": "gC", "sp": "sL"}
SPECIAL = ("pQ", "pW", "vS", "pD", "sE")
ALLSEMS = ("pC", "vC", "sC", "gC", "sL", "gL") + SPECIAL


class Tok:
    __slots__ = ("eng", "key", "sem", "val")

    def __init__(self, eng, key, sem=None):
        self.eng = eng
        self.key = key
        self.sem = sem
        self.val = None


class Plan:
    def __init__(self):
        self.lists = {e: [] for e in ENGS}
        self.var = [{e: [] for e in ("pe", "ve", "sc")} for _ in range(2)]
        self.mode = None
        self.replay = None
        self.record = None
        self.cuts = None

    def op(self, eng, emit, waits=(), sem=None, dma=False):
        waits = tuple(w for w in waits if w is not None)
        if self.mode == 1 and eng in ("sp", "gp"):
            return self.replay.pop(0)
        if self.mode is not None and eng in ("pe", "ve", "sc"):
            lname, lst = f"v{self.mode}", self.var[self.mode][eng]
        else:
            lname, lst = "main", self.lists[eng]
        if dma:
            sem = "gL" if eng == "gp" else "sL"
        tok = Tok(eng, (lname, eng, len(lst)), sem=sem)
        lst.append([waits, emit, tok, dma])
        if self.mode == 0 and eng in ("sp", "gp"):
            self.record.append(tok)
        return tok

    def dma(self, eng, out, in_, waits=()):
        return self.op(eng, lambda h, o=out, i=in_: h.dma_start(out=o, in_=i),
                       waits=waits, dma=True)

    def tdma(self, out, in_, waits=()):
        return self.op("sp", lambda h, o=out, i=in_: h.dma_start_transpose(out=o, in_=i),
                       waits=waits, dma=True)

    def begin_variant(self, v):
        self.mode = v
        if v == 0:
            self.record = []
        else:
            self.replay = list(self.record)

    def end_variant(self):
        self.mode = None

    def resolve(self):
        used = set()
        groups = [(e, self.lists[e]) for e in ENGS]
        for v in range(2):
            groups += [(e, self.var[v][e]) for e in ("pe", "ve", "sc")]
        for e, ops in groups:
            for waits, emit, tok, dma in ops:
                for w in waits:
                    if w.eng != e or e == "gp":
                        used.add(w.key)
        cnt = {s: 0 for s in ALLSEMS}

        def assign(eng, ops):
            for waits, emit, tok, dma in ops:
                if dma or tok.sem is not None:
                    sem = tok.sem or DEFSEM[eng]
                    tok.sem = sem
                    cnt[sem] += 16 if dma else 1
                    tok.val = cnt[sem]
                elif tok.key in used:
                    tok.sem = DEFSEM[eng]
                    cnt[tok.sem] += 1
                    tok.val = cnt[tok.sem]

        for e in ("sp", "gp"):
            assign(e, self.lists[e])
        for e in ("pe", "ve", "sc"):
            cut = self.cuts[e]
            assign(e, self.lists[e][:cut])
            snap = dict(cnt)
            assign(e, self.var[0][e])
            cnt0 = dict(cnt)
            for s in ALLSEMS:
                cnt[s] = snap[s]
            assign(e, self.var[1][e])
            for s in ALLSEMS:
                assert cnt[s] == cnt0[s], ("variant sem mismatch", e, s, cnt[s], cnt0[s])
            assign(e, self.lists[e][cut:])
        self.final = cnt


class Slot:
    def __init__(self, init=()):
        self.frees = list(init)

    def take(self):
        f = self.frees
        self.frees = []
        return f


def ring(n, init=()):
    return [Slot(init) for _ in range(n)]


def build(debug=0):
    nc = bass.Bass("TRN2", target_bir_lowering=False, debug=False)
    plan = Plan()
    ctx0 = ExitStack()

    def mk_sb(stack):
        def f(name, shape, dt=BF16, side=None):
            kw = {"side": side} if side else {}
            return stack.enter_context(nc.sbuf_tensor(name, shape, dt, **kw))
        return f

    sb0 = mk_sb(ctx0)
    psr = lambda name, shape, dt=F32: ctx0.enter_context(nc.psum_tensor(name, shape, dt))

    x_seq = nc.dram_tensor("x_seq", [T, D], BF16, kind="ExternalInput").ap()
    x_own = nc.dram_tensor("x_own", [NOB * TB, D], BF16, kind="ExternalInput").ap()
    wq_d = nc.dram_tensor("wq", [NH, 128, D], BF16, kind="ExternalInput").ap()
    wk_d = nc.dram_tensor("wk", [128, NCH, 512], BF16, kind="ExternalInput").ap()
    wv_d = nc.dram_tensor("wv", [128, NCH, 512], BF16, kind="ExternalInput").ap()
    wo_d = nc.dram_tensor("wo", [4, 4, 128, D], BF16, kind="ExternalInput").ap()
    bq_d = nc.dram_tensor("bq", [1, D], BF16, kind="ExternalInput").ap()
    bk_d = nc.dram_tensor("bk", [1, 512], BF16, kind="ExternalInput").ap()
    bv_d = nc.dram_tensor("bv", [1, 512], BF16, kind="ExternalInput").ap()
    wg_d = nc.dram_tensor("wg", [NIC, 128, D], F32, kind="ExternalInput").ap()
    wu_d = nc.dram_tensor("wu", [NIC, 128, D], F32, kind="ExternalInput").ap()
    wd_d = nc.dram_tensor("wd", [NIC, 128, D], F32, kind="ExternalInput").ap()
    y_d = nc.dram_tensor("y", [NOB * TB, D], F32, kind="ExternalOutput").ap()
    s_scr = nc.dram_tensor("s_scr", [NOB * TB, D], BF16, kind="Internal").ap()
    wgb_d = nc.dram_tensor("wgb", [NIC, 128, D], BF16, kind="Internal").ap()
    wub_d = nc.dram_tensor("wub", [NIC, 128, D], BF16, kind="Internal").ap()
    wdb_d = nc.dram_tensor("wdb", [NIC, 128, D], BF16, kind="Internal").ap()

    dbg = {}
    if debug >= 1:
        for nm, shape in (("dbg_xnT", [128, NOB * NCH * TB]), ("dbg_kT", [128, NKV * T]),
                          ("dbg_v", [128, (T // 128) * 512]), ("dbg_cos", [128, T]),
                          ("dbg_sin", [128, T])):
            dbg[nm] = nc.dram_tensor(nm, shape, BF16, kind="ExternalOutput").ap()
    if debug >= 2:
        dbg["dbg_s"] = nc.dram_tensor("dbg_s", [NOB * TB, D], BF16,
                                      kind="ExternalOutput").ap()

    ones_bf = sb0("ones_bf", [128, 1])
    fen_v = sb0("fen_v", [1, 4], F32)
    fen_s = sb0("fen_s", [1, 4], F32)
    ones1f = sb0("ones1f", [1, 128], F32)
    ones1b = sb0("ones1b", [1, 128])
    invf = sb0("invf", [128, 1], F32)
    epsb = sb0("epsb", [128, 1], F32)
    m0 = sb0("m0", [128, 896])
    bq_c = sb0("bq_c", [128, NH], F32)
    bk_c = sb0("bk_c", [128, NKV], F32)
    bv_c = sb0("bv_c", [128, NKV], F32)
    kraw = [sb0(f"kraw{i}", [128, TB]) for i in range(2)]
    rope_r = sb0("rope_r", [128, TB])
    rope_a = sb0("rope_a", [128, TB])
    rope_b = sb0("rope_b", [128, TB])
    xnT_own = sb0("xnT_own", [128, NOB * NCH * TB])

    ctx12 = ExitStack()
    sb12 = mk_sb(ctx12)
    cosT = sb12("cosT", [128, T], side="right")
    sinS = sb12("sinS", [128, T], side="right")
    kT = sb12("kT", [128, NKV * T], side="right")
    v_sb = sb12("v_sb", [128, (T // 128) * 512], side="right")

    pball = psr("pball", [128, 4096])
    pbank = [pball[:, i * 512:(i + 1) * 512] for i in range(8)]
    psfree = {i: [] for i in range(8)}

    ctxp0 = ExitStack()
    sbp = mk_sb(ctxp0)
    browb = sbp("browb", [1, D])
    frow = sbp("frow", [1, 128], F32)
    tsc = [sbp(f"tsc{i}", [128, 512], F32) for i in range(3)]
    tsi = sbp("tsi", [128, 512], I32)
    trow = sbp("trow", [1, T], F32)

    t_on = plan.op("gp", lambda h: h.memset(ones_bf[:], 1.0), sem="gC")
    t_eps = plan.op("gp", lambda h: h.memset(epsb[:], 1e-5), sem="gC")
    t_o1f = plan.op("gp", lambda h: h.memset(ones1f[:], 1.0), sem="gC")
    t_on2 = plan.op("gp", lambda h: h.memset(ones1b[:], 1.0), sem="gC")
    t_m0a = plan.op("gp", lambda h: h.memset(m0[:], 1.0), sem="gC")
    t_m0 = plan.op("gp", lambda h: h.affine_select(
        m0[:], m0[:], [[1, 896]], ALU.is_ge, 0.0, base=-384, channel_multiplier=-1),
        waits=[t_m0a], sem="gC")
    t_fr = plan.op("gp", lambda h: h.iota(
        frow[:].rearrange("a (b c) -> a b c", c=64), [[0, 2], [1, 64]], base=0,
        channel_multiplier=0, allow_small_or_imprecise_dtypes=True), sem="gC")
    t_tr = plan.op("gp", lambda h: h.iota(
        trow[:], [[1, T]], base=0, channel_multiplier=0,
        allow_small_or_imprecise_dtypes=True), sem="gC")
    t_ifr = plan.op("sc", lambda h: h.activation(
        frow[:], frow[:], AF.Exp, scale=-LN1E6 / 64.0), waits=[t_fr])
    t_bcf = plan.op("pe", lambda h: h.matmul(
        pbank[0][:, 0:1], frow[:], ones1f[0:1, 0:1], start=True, stop=True),
        waits=[t_ifr, t_o1f])
    plan.op("ve", (lambda h: h.nop(cycle_cnt=DELAY)) if DELAY else
            (lambda h: h.tensor_copy(fen_v[:], ones1f[0:1, 0:4])), waits=[t_bcf])
    t_if = plan.op("ve", lambda h: h.tensor_copy(invf[:], pbank[0][:, 0:1]),
                   waits=[t_bcf])

    cos_last = sin_last = None
    t_an_prev = None
    for b in range(T // 512):
        sl = slice(b * 512, (b + 1) * 512)
        t_bc = plan.op("pe", lambda h, sl=sl: h.matmul(
            pbank[0][:], ones1f[:], trow[0:1, sl], start=True, stop=True),
            waits=[t_tr, t_if, t_an_prev, t_o1f])
        plan.op("ve", (lambda h: h.nop(cycle_cnt=DELAY)) if DELAY else
                (lambda h: h.tensor_copy(fen_v[:], ones1f[0:1, 0:4])), waits=[t_bc])
        t_an = plan.op("ve", lambda h: h.tensor_scalar(
            tsc[0][:], pbank[0][:], invf[:], None, ALU.mult), waits=[t_bc])
        t_an_prev = t_an
        for table in (cosT, sinS):
            off = PI / 2 if table is cosT else 0.0
            plan.op("ve", lambda h, off=off: h.tensor_scalar(
                tsc[1][:], tsc[0][:], off, 1.0 / (2 * PI), ALU.add, ALU.mult))
            plan.op("ve", lambda h: h.tensor_copy(tsi[:], tsc[1][:]))
            plan.op("ve", lambda h: h.tensor_copy(tsc[1][:], tsi[:]))
            plan.op("ve", lambda h, off=off: h.tensor_single_scalar(
                tsc[2][:], tsc[0][:], off, ALU.add))
            plan.op("ve", lambda h: h.scalar_tensor_tensor(
                tsc[2][:], tsc[1][:], -TWO_PI_HI, tsc[2][:], ALU.mult, ALU.add))
            plan.op("ve", lambda h: h.scalar_tensor_tensor(
                tsc[2][:], tsc[1][:], -TWO_PI_LO, tsc[2][:], ALU.mult, ALU.add))
            plan.op("ve", lambda h: h.tensor_single_scalar(
                tsc[1][:], tsc[2][:], PI, ALU.is_gt))
            t_r3 = plan.op("ve", lambda h: h.scalar_tensor_tensor(
                tsc[2][:], tsc[1][:], -TWO_PI_HI, tsc[2][:], ALU.mult, ALU.add))
            if table is cosT:
                t_s = plan.op("sc", lambda h, sl=sl: h.activation(
                    cosT[:, sl], tsc[2][:], AF.Sin), waits=[t_r3])
                cos_last = t_s
                plan.op("ve", lambda h: h.engine_nop(), waits=[t_s])
            else:
                t_ng = plan.op("ve", lambda h: h.tensor_single_scalar(
                    tsc[1][0:64, :], tsc[2][0:64, :], -1.0, ALU.mult))
                t_s1 = plan.op("sc", lambda h, sl=sl: h.activation(
                    sinS[0:64, sl], tsc[1][0:64, :], AF.Sin), waits=[t_ng])
                t_s2 = plan.op("sc", lambda h, sl=sl: h.activation(
                    sinS[64:128, sl], tsc[2][64:128, :], AF.Sin), waits=[t_r3])
                sin_last = t_s2
                plan.op("ve", lambda h: h.engine_nop(), waits=[t_s1, t_s2])
    psfree[0] = [t_an_prev]
    t_cos, t_sin = cos_last, sin_last

    def bias_chunks(bias_d, n_ch, dst, bank, prev=None):
        t_l = plan.dma("sp", browb[0:1, 0:n_ch * 128], bias_d[0:1, :],
                       waits=[prev] if prev else [])
        mm = None
        for c in range(n_ch):
            mm = plan.op("pe", lambda h, c=c, bank=bank, n=n_ch: h.matmul(
                pbank[bank][:, c:c + 1], browb[0:1, c * 128:(c + 1) * 128],
                ones1b[0:1, 0:1], start=(c == 0), stop=(c == n_ch - 1)),
                waits=[t_l, t_on2] + (psfree[bank] if c == 0 else []))
        plan.op("sc", (lambda h: h.nop(cycle_cnt=DELAY)) if DELAY else
                (lambda h: h.copy(fen_s[:], ones1f[0:1, 0:4])), waits=[mm])
        return plan.op("sc", lambda h, dst=dst, bank=bank, n=n_ch: h.copy(
            dst[:], pbank[bank][:, 0:n_ch]), waits=[mm]), mm

    t_bq, mm_q = bias_chunks(bq_d, NH, bq_c, 1)
    t_bk, mm_k = bias_chunks(bk_d, NKV, bk_c, 2, prev=mm_q)
    t_bv, t_bias_mm_last = bias_chunks(bv_d, NKV, bv_c, 3, prev=mm_k)

    conv_g = [plan.dma("gp", wgb_d[ic], wg_d[ic]) for ic in range(NIC)]
    conv_u = [plan.dma("gp", wub_d[ic], wu_d[ic]) for ic in range(NIC)]
    conv_d_toks = []
    psfree[1], psfree[2], psfree[3] = [t_bq], [t_bk], [t_bv]
    if P0_CLOSE:
        ctxp0.close()

    ctx1 = ExitStack()
    sb1 = mk_sb(ctx1)
    xTf = [sb1(f"xTf{i}", [128, TB]) for i in range(2)]
    sqt = [sb1(f"sqt{i}", [128, TB]) for i in range(2)]
    st_mean = sb1("st_mean", [1, TB], F32)
    st_msq = sb1("st_msq", [1, TB], F32)
    st_var = sb1("st_var", [1, TB], F32)
    st_rstd = sb1("st_rstd", [1, TB], F32)
    st_b = sb1("st_b", [1, TB], F32)
    ab_sb = sb1("ab_sb", [128, 2 * TB])
    xnc = [sb1(f"xnc{i}", [128, TB]) for i in range(2)]
    wk_sb = sb1("wk_sb", [128, NCH * 512])
    wv_sb = sb1("wv_sb", [128, NCH * 512])

    t_p0_pe = t_bias_mm_last
    t_p0_ve = plan.op("ve", lambda h: h.tensor_copy(fen_v[:], ones1f[0:1, 0:4]),
                      sem="vC")
    t_p0_sc = plan.op("sc", lambda h: h.copy(fen_s[:], ones1f[0:1, 0:4]),
                      sem="sC")
    G0 = [t_p0_pe, t_p0_ve, t_p0_sc]

    t_wk = plan.dma("sp", wk_sb[:].rearrange("p (c n) -> p c n", n=512), wk_d[:],
                    waits=G0)
    t_wv = plan.dma("sp", wv_sb[:].rearrange("p (c n) -> p c n", n=512), wv_d[:])

    last_pe_p1 = t_bias_mm_last
    xTf_r = ring(2, G0)
    xnc_r = ring(2)
    sq_r = ring(2, G0)
    kraw_r = ring(2)
    ab_s = Slot(G0)

    for blk in range(min(NB + NOB, P1_BLOCKS)):
        own = blk >= NB
        t0 = (blk - NB) * TB if own else blk * TB
        src = x_own if own else x_seq
        mm_sum = sq_mm = None
        for c in range(NCH):
            sx = xTf_r[c % 2]
            t_x = plan.tdma(xTf[c % 2][:],
                            src[t0:t0 + TB, c * 128:(c + 1) * 128], waits=sx.take())
            mm_sum = plan.op("pe", lambda h, c=c, i=c % 2: h.matmul(
                pbank[0][0:1, :], ones_bf[:], xTf[i][:],
                start=(c == 0), stop=(c == NCH - 1)),
                waits=[t_x, t_on] + (psfree[0] if c == 0 else []))
            sq = sq_r[c % 2]
            t_sq = plan.op("sc", lambda h, i=c % 2: h.activation(
                sqt[i][:], xTf[i][:], AF.Square), waits=[t_x] + sq.take())
            sq_mm = plan.op("pe", lambda h, c=c, i=c % 2: h.matmul(
                pbank[1][0:1, :], ones_bf[:], sqt[i][:],
                start=(c == 0), stop=(c == NCH - 1)),
                waits=[t_sq] + (psfree[1] if c == 0 else []))
            sq.frees.append(sq_mm)
            sx.frees += [mm_sum, t_sq]
        if P1_SUB < 2:
            psfree[0] = [plan.op("ve", lambda h: h.engine_nop(), waits=[mm_sum])]
            psfree[1] = [plan.op("ve", lambda h: h.engine_nop(), waits=[sq_mm])]
            continue
        last_pe_p1 = sq_mm
        plan.op("ve", (lambda h: h.nop(cycle_cnt=DELAY)) if DELAY else
                (lambda h: h.tensor_copy(fen_v[:], ones1f[0:1, 0:4])),
                waits=[mm_sum, sq_mm])
        t_mean = plan.op("ve", lambda h: h.tensor_single_scalar(
            st_mean[:], pbank[0][0:1, :], 1.0 / D, ALU.mult), waits=[mm_sum])
        if P1_FINE < 2:
            psfree[0] = [t_mean]
            psfree[1] = [plan.op("ve", lambda h: h.tensor_single_scalar(
                st_msq[:], pbank[1][0:1, :], 1.0 / D, ALU.mult), waits=[sq_mm])]
            continue
        t_msq = plan.op("ve", lambda h: h.tensor_single_scalar(
            st_msq[:], pbank[1][0:1, :], 1.0 / D, ALU.mult), waits=[sq_mm])
        psfree[0] = [t_mean]
        psfree[1] = [t_msq]
        plan.op("ve", lambda h: h.tensor_tensor(
            st_b[:], st_mean[:], st_mean[:], ALU.mult))
        t_var = plan.op("ve", lambda h: h.tensor_tensor(
            st_var[:], st_msq[:], st_b[:], ALU.subtract))
        if P1_FINE < 3:
            psfree[0] = [t_mean]
            psfree[1] = [t_msq]
            continue
        t_ln = plan.op("sc", lambda h: h.activation(
            st_var[:], st_var[:], AF.Ln, bias=epsb[0:1, :]), waits=[t_var, t_eps])
        t_rstd = plan.op("sc", lambda h: h.activation(
            st_rstd[:], st_var[:], AF.Exp, scale=-0.5), waits=[t_ln])
        if P1_FINE < 4:
            psfree[0] = [t_mean]
            psfree[1] = [t_msq]
            continue
        t_b2 = plan.op("ve", lambda h: h.scalar_tensor_tensor(
            st_b[:], st_mean[:], -1.0, st_rstd[:], ALU.mult, ALU.mult),
            waits=[t_rstd])
        if P1_FINE < 5:
            psfree[0] = [t_mean]
            psfree[1] = [t_msq]
            continue
        t_bca = plan.op("pe", lambda h: h.matmul(
            pbank[2][:], ones1f[:], st_rstd[:], start=True, stop=True),
            waits=[t_rstd] + psfree[2])
        t_bcb = plan.op("pe", lambda h: h.matmul(
            pbank[3][:], ones1f[:], st_b[:], start=True, stop=True),
            waits=[t_b2] + psfree[3])
        if P1_FINE < 6:
            psfree[2] = [plan.op("ve", lambda h: h.engine_nop(), waits=[t_bca])]
            psfree[3] = [plan.op("ve", lambda h: h.engine_nop(), waits=[t_bcb])]
            if P1_FINE == 5.5:
                plan.op("sc", lambda h: h.copy(st_b[:], st_mean[:]))
                plan.op("sc", lambda h: h.copy(st_b[:], st_mean[:]))
            if P1_FINE == 5.7:
                plan.op("ve", lambda h: h.tensor_copy(ab_sb[:, 0:TB], pbank[2][:]),
                        waits=[t_bca] + ab_s.take())
                plan.op("ve", lambda h: h.tensor_copy(ab_sb[:, TB:2 * TB], pbank[3][:]),
                        waits=[t_bcb])
            if P1_FINE == 5.8:
                plan.op("sc", lambda h: h.copy(sqt[0][:], pbank[2][:]),
                        waits=[t_bca])
                plan.op("sc", lambda h: h.copy(sqt[1][:], pbank[3][:]),
                        waits=[t_bcb])
            if P1_FINE == 5.81:
                plan.op("sc", lambda h: h.copy(sqt[0][:], pbank[2][:]),
                        waits=[t_bca])
            if P1_FINE == 5.82:
                plan.op("sc", lambda h: h.copy(sqt[0][:], pbank[3][:]),
                        waits=[t_bcb])
            if P1_FINE == 5.83:
                plan.op("sc", lambda h: h.copy(sqt[0][:, 0:4], pbank[2][:, 0:4]),
                        waits=[t_bca])
            if P1_FINE == 5.84:
                plan.op("sc", lambda h: h.copy(sqt[0][:], pbank[2][:]),
                        waits=[t_bca, t_bcb])
            continue
        abfr = ab_s.take()
        plan.op("sc", (lambda h: h.nop(cycle_cnt=DELAY)) if DELAY else
                (lambda h: h.copy(fen_s[:], ones1f[0:1, 0:4])), waits=[t_bcb])
        t_ea = plan.op("sc", lambda h: h.copy(ab_sb[:, 0:TB], pbank[2][:]),
                       waits=[t_bca] + abfr)
        t_eb = plan.op("sc", lambda h: h.copy(ab_sb[:, TB:2 * TB], pbank[3][:]),
                       waits=[t_bcb])
        psfree[2], psfree[3] = [t_ea], [t_eb]

        if P1_SUB < 3:
            continue
        kv_bank_wait = [[t_mean], [t_msq], [t_ea], [t_eb]]
        last_mm_per_chunk = {}
        xn_last = None
        for c in range(NCH):
            sx = xTf_r[c % 2]
            t_x2 = plan.tdma(xTf[c % 2][:],
                             src[t0:t0 + TB, c * 128:(c + 1) * 128], waits=sx.take())
            xs = xnc_r[c % 2]
            if own:
                o = ((blk - NB) * NCH + c) * TB
                xdst = xnT_own[:, o:o + TB]
            else:
                xdst = xnc[c % 2][:]
            plan.op("ve", lambda h, i=c % 2: h.tensor_tensor(
                rope_r[:], xTf[i][:], ab_sb[:, 0:TB], ALU.mult),
                waits=[t_x2, t_ea, t_eb] + (xs.take() if not own else []))
            xn_last = plan.op("ve", lambda h, xd=xdst: h.tensor_tensor(
                xd, rope_r[:], ab_sb[:, TB:2 * TB], ALU.add))
            sx.frees.append(xn_last)
            ab_s.frees.append(xn_last)
            if own or P1_SUB < 4:
                continue
            mm = None
            for kv in range(NKV):
                mm = plan.op("pe", lambda h, c=c, kv=kv, i=c % 2: h.matmul(
                    pbank[kv][:],
                    wk_sb[:].rearrange("p (c n) -> p c n", n=512)[:, c, kv * 128:(kv + 1) * 128],
                    xnc[i][:], start=(c == 0), stop=(c == NCH - 1)),
                    waits=[xn_last] + ([t_wk] + kv_bank_wait[kv] + psfree[kv]
                                       if c == 0 else []))
            for ts in range(4):
                mm = plan.op("pe", lambda h, c=c, ts=ts, i=c % 2: h.matmul(
                    pbank[4 + ts][:], xnc[i][:, ts * 128:(ts + 1) * 128],
                    wv_sb[:].rearrange("p (c n) -> p c n", n=512)[:, c, :],
                    start=(c == 0), stop=(c == NCH - 1)),
                    waits=([t_wv] + psfree[4 + ts] if c == 0 else []))
            xs.frees.append(mm)
        if own or P1_SUB < 5:
            continue

        t_kT = None
        plan.op("ve", (lambda h: h.nop(cycle_cnt=DELAY)) if DELAY else
                (lambda h: h.tensor_copy(fen_v[:], ones1f[0:1, 0:4])), waits=[mm])
        for kv in range(NKV):
            kslot = kraw_r[kv % 2]
            t_kr = plan.op("ve", lambda h, kv=kv, i=kv % 2: h.tensor_scalar(
                kraw[i][:], pbank[kv][:], bk_c[:, kv:kv + 1], None, ALU.add),
                waits=[mm, t_bk] + kslot.take())
            psfree[kv] = [t_kr]
            plan.op("ve", lambda h, i=kv % 2: h.tensor_copy(
                rope_r[0:64, :], kraw[i][64:128, :]), waits=[t_kr])
            plan.op("ve", lambda h, i=kv % 2: h.tensor_copy(
                rope_r[64:128, :], kraw[i][0:64, :]))
            plan.op("ve", lambda h, i=kv % 2, t0=t0: h.tensor_tensor(
                rope_a[:], kraw[i][:], cosT[:, t0:t0 + TB], ALU.mult),
                waits=[t_cos])
            plan.op("ve", lambda h, t0=t0: h.tensor_tensor(
                rope_b[:], rope_r[:], sinS[:, t0:t0 + TB], ALU.mult),
                waits=[t_sin])
            t_kT = plan.op("ve", lambda h, kv=kv, t0=t0: h.tensor_tensor(
                kT[:, kv * T + t0:kv * T + t0 + TB], rope_a[:], rope_b[:], ALU.add))
            kslot.frees.append(t_kT)
        t_v = None
        last_pe_p1 = mm
        for ts in range(4):
            tt = blk * 4 + ts
            t_v = plan.op("ve", lambda h, tt=tt, ts=ts: h.tensor_copy(
                v_sb[:, tt * 512:(tt + 1) * 512], pbank[4 + ts][:]), waits=[mm])
            psfree[4 + ts] = [t_v]

    t_p1_ve = plan.op("ve", lambda h: h.tensor_copy(fen_v[:], ones1f[0:1, 0:4]),
                      waits=[last_pe_p1], sem="vC")
    t_p1_sc = plan.op("sc", lambda h: h.copy(fen_s[:], ones1f[0:1, 0:4]),
                      waits=[last_pe_p1], sem="sC")
    t_p1_pe = last_pe_p1
    G1 = [t_p1_pe, t_p1_ve, t_p1_sc]

    if debug >= 1:
        plan.dma("gp", dbg["dbg_xnT"], xnT_own[:], waits=G1)
        plan.dma("gp", dbg["dbg_kT"], kT[:], waits=G1)
        plan.dma("gp", dbg["dbg_v"], v_sb[:], waits=G1)
        plan.dma("gp", dbg["dbg_cos"], cosT[:], waits=G1)
        plan.dma("gp", dbg["dbg_sin"], sinS[:], waits=G1)
        dbg["dbg_bias"] = nc.dram_tensor("dbg_bias", [128, 24], F32,
                                         kind="ExternalOutput").ap()
        if debug >= 2:
            dbg["dbg_oT"] = nc.dram_tensor("dbg_oT", [128, NH * TB], BF16,
                                           kind="ExternalOutput").ap()
        plan.dma("gp", dbg["dbg_bias"][:, 0:16], bq_c[:], waits=G1)
        plan.dma("gp", dbg["dbg_bias"][:, 16:20], bk_c[:], waits=G1)
        plan.dma("gp", dbg["dbg_bias"][:, 20:24], bv_c[:], waits=G1)

    ctx1.close()
    plan.cuts = {e: len(plan.lists[e]) for e in ENGS}
    stop_after_p1 = (debug >= 9)

    meta = dict(nc=nc, ctx0=ctx0, ctx12=ctx12, plan=plan, dbg=dbg,
                stop_after_p1=stop_after_p1)
    if stop_after_p1:
        ctx12.close()
        return meta

    ctx2 = ExitStack()
    sb2 = mk_sb(ctx2)
    wq_t = [sb2(f"wq_t{i}", [128, D]) for i in range(1)]
    wo_t = [sb2(f"wo_t{i}", [128, D]) for i in range(4)]
    qT_t = [sb2(f"qT_t{i}", [128, TB]) for i in range(2)]
    pT_t = [sb2(f"pT_t{i}", [128, TB]) for i in range(4)]
    pexp = sb2("pexp", [128, TB])
    row_s = sb2("row_s", [1, TB], F32)
    bc_s = sb2("bc_s", [128, TB])
    ot1 = sb2("ot1", [128, TB])
    oT = sb2("oT", [128, NH * TB])
    xo_t = [sb2(f"xo_t{i}", [128, TB]) for i in range(4)]
    s_st = [sb2(f"s_st{i}", [128, TB]) for i in range(4)]

    p2_psfree0 = {i: list(psfree[i]) for i in range(8)}
    kraw_init = [list(s.frees) for s in kraw_r]
    sstore_tok = {}
    t_ss_last_box = [None]

    def plan_p2(v):
        plan.begin_variant(v)
        psf = {i: list(p2_psfree0[i]) for i in range(8)}
        wq_r = ring(1, G1)
        wo_r = ring(4, G1)
        qT_r = ring(2)
        pT_r = ring(4)
        pexp_s = Slot()
        xo_r = ring(4, G1)
        sst_r = ring(4, G1)
        kr_r = [Slot(kraw_init[0]), Slot(kraw_init[1])]
        oT_s = Slot()
        rows_s = Slot()
        bc_slot = Slot()
        ot1_s = Slot()
        q0s = Q0S[v]
        t_oT_last = None
        pend = [None]

        def tail_finish():
            if pend[0] is None:
                return None
            h_p, last_pv_p, t_rc_p, pv_b = pend[0]
            pend[0] = None
            kv_p = h_p // 4
            t_bcm = plan.op("pe", lambda h: h.matmul(
                pbank[5][:], ones1f[:], row_s[:], start=True, stop=True),
                waits=[t_rc_p] + psf[5])
            rows_s.frees.append(t_bcm)
            plan.op("sc", (lambda h: h.nop(cycle_cnt=DELAY)) if DELAY else
                    (lambda h: h.copy(fen_s[:], ones1f[0:1, 0:4])),
                    waits=[t_bcm])
            t_bcc = plan.op("sc", lambda h: h.copy(bc_s[:], pbank[5][:]),
                            waits=[t_bcm] + bc_slot.take())
            psf[5] = [t_bcc]
            t_o1 = plan.op("ve", lambda h, b=pv_b: h.tensor_tensor(
                ot1[:], pbank[b][:], bc_s[:], ALU.mult),
                waits=[last_pv_p, t_bcc] + ot1_s.take())
            psf[pv_b] = [t_o1]
            bc_slot.frees.append(t_o1)
            t_oT = plan.op("ve", lambda h, h_i=h_p, kv=kv_p: h.tensor_scalar(
                oT[:, h_i * TB:(h_i + 1) * TB], ot1[:], bv_c[:, kv:kv + 1],
                None, ALU.add),
                waits=[t_o1] + (oT_s.take() if h_p == 0 else []))
            ot1_s.frees.append(t_oT)
            return t_oT

        for qb in range(NQB):
            q0 = q0s[qb]
            n_kt = (q0 + TB) // 128
            for h_i in range(NH):
                wqs = wq_r[0]
                t_wq = plan.dma("sp", wq_t[0][:], wq_d[h_i], waits=wqs.take())
                ic_conv = qb * NH + h_i
                t_cd = plan.dma("gp", wdb_d[ic_conv], wd_d[ic_conv])
                if v == 0:
                    conv_d_toks.append(t_cd)
                mm = None
                for c in range(NCH):
                    mm = plan.op("pe", lambda h, c=c, i=0, qb=qb: h.matmul(
                        pbank[2][:], wq_t[i][:, c * 128:(c + 1) * 128],
                        xnT_own[:, (qb * NCH + c) * TB:(qb * NCH + c + 1) * TB],
                        start=(c == 0), stop=(c == NCH - 1)),
                        waits=[t_wq] + ([t_p1_ve] + psf[2] if c == 0 else []),
                        sem="pQ" if c == NCH - 1 else None)
                wqs.frees.append(mm)
                krs = kr_r[h_i % 2]
                plan.op("ve", (lambda h: h.nop(cycle_cnt=DELAY)) if DELAY else
                        (lambda h: h.tensor_copy(fen_v[:], ones1f[0:1, 0:4])),
                        waits=[mm])
                t_qr = plan.op("ve", lambda h, h_i=h_i, i=h_i % 2: h.tensor_scalar(
                    kraw[i][:], pbank[2][:], bq_c[:, h_i:h_i + 1], None, ALU.add),
                    waits=[mm] + krs.take())
                psf[2] = [t_qr]
                qts = qT_r[h_i % 2]
                qfr = qts.take()
                plan.op("ve", lambda h, i=h_i % 2: h.tensor_copy(
                    rope_r[0:64, :], kraw[i][64:128, :]), waits=[t_qr] + qfr)
                plan.op("ve", lambda h, i=h_i % 2: h.tensor_copy(
                    rope_r[64:128, :], kraw[i][0:64, :]))
                plan.op("ve", lambda h, i=h_i % 2, q0=q0: h.tensor_tensor(
                    rope_a[:], kraw[i][:], cosT[:, q0:q0 + TB], ALU.mult))
                plan.op("ve", lambda h, q0=q0: h.tensor_tensor(
                    rope_b[:], rope_r[:], sinS[:, q0:q0 + TB], ALU.mult))
                t_q = plan.op("ve", lambda h, i=h_i % 2: h.tensor_tensor(
                    qT_t[i][:], rope_a[:], rope_b[:], ALU.add))
                krs.frees.append(t_q)
                t_pt = tail_finish()
                if t_pt is not None:
                    t_oT_last = t_pt

                kv = h_i // 4
                last_pv = last_row = None
                t_ps = [None] * n_kt
                for kt in range(n_kt + 1):
                    if kt < n_kt:
                        sbk = kt % 2
                        t_s = plan.op("pe", lambda h, kt=kt, kv=kv, i=h_i % 2, sbk=sbk: h.matmul(
                            pbank[sbk][:], kT[:, kv * T + kt * 128:kv * T + (kt + 1) * 128],
                            qT_t[i][:], start=True, stop=True),
                            waits=[t_q] + psf[sbk])
                        pslot = pT_r[kt % 4]
                        pfr = pslot.take()
                        if kt >= n_kt - 4:
                            t_e = plan.op("sc", lambda h, sbk=sbk: h.activation(
                                pexp[:], pbank[sbk][:], AF.Exp, scale=SCALE),
                                waits=[t_s] + pexp_s.take())
                            dd = kt - (n_kt - 4)
                            t_p = plan.op("ve", lambda h, kt=kt, dd=dd: h.tensor_tensor(
                                pT_t[kt % 4][:], pexp[:], m0[:, 384 - 128 * dd:896 - 128 * dd],
                                ALU.mult), waits=[t_e, t_m0] + pfr)
                            pexp_s.frees.append(t_p)
                        else:
                            t_p = plan.op("sc", lambda h, kt=kt, sbk=sbk: h.activation(
                                pT_t[kt % 4][:], pbank[sbk][:], AF.Exp, scale=SCALE),
                                waits=[t_s] + pfr)
                        psf[sbk] = [t_p]
                        t_ps[kt] = t_p
                    if kt > 0:
                        ku = kt - 1
                        rs_b = 3 if h_i % 2 == 0 else 7
                        pv_b = 4 if h_i % 2 == 0 else 6
                        last_row = plan.op("pe", lambda h, ku=ku, n=n_kt, b=rs_b: h.matmul(
                            pbank[b][0:1, :], ones_bf[:], pT_t[ku % 4][:],
                            start=(ku == 0), stop=(ku == n - 1)),
                            waits=[t_ps[ku]] + (psf[rs_b] if ku == 0 else []))
                        last_pv = plan.op("pe", lambda h, ku=ku, kv=kv, n=n_kt, b=pv_b: h.matmul(
                            pbank[b][:], v_sb[:, ku * 512 + kv * 128:ku * 512 + (kv + 1) * 128],
                            pT_t[ku % 4][:], start=(ku == 0), stop=(ku == n - 1)),
                            waits=(psf[pv_b] if ku == 0 else []))
                        pT_r[ku % 4].frees.append(last_pv)
                qts.frees.append(last_pv)
                rs_b = 3 if h_i % 2 == 0 else 7
                plan.op("ve", (lambda h: h.nop(cycle_cnt=DELAY)) if DELAY else
                        (lambda h: h.tensor_copy(fen_v[:], ones1f[0:1, 0:4])),
                        waits=[last_row])
                t_rc = plan.op("ve", lambda h, b=rs_b: h.reciprocal(
                    row_s[:], pbank[b][0:1, :]),
                    waits=[last_row] + rows_s.take())
                psf[rs_b] = [t_rc]
                pend[0] = (h_i, last_pv, t_rc, 4 if h_i % 2 == 0 else 6)
            t_pt = tail_finish()
            if t_pt is not None:
                t_oT_last = t_pt

            for db in range(4):
                g_toks = [None] * 4
                for g in range(4):
                    g_toks[g] = plan.dma("sp", wo_t[g][:], wo_d[db, g],
                                         waits=wo_r[g].take())
                for ts in range(4):
                    xos = xo_r[(db * 4 + ts) % 4]
                    t_xo = plan.dma("sp", xo_t[(db * 4 + ts) % 4][:],
                                    x_own[qb * TB + ts * 128:qb * TB + (ts + 1) * 128,
                                          db * 512:(db + 1) * 512],
                                    waits=xos.take())
                    bank = 6 + (db * 4 + ts) % 2
                    xi4 = (db * 4 + ts) % 4
                    mm = None
                    for h_i in range(NH):
                        g = h_i // 4
                        w = [t_oT_last] + (psf[bank] if h_i == 0 else [])
                        if ts == 0 and h_i % 4 == 0:
                            w.append(g_toks[g])
                        mm = plan.op("pe", lambda h, h_i=h_i, ts=ts, g=g, bank=bank: h.matmul(
                            pbank[bank][:],
                            oT[:, h_i * TB + ts * 128:h_i * TB + (ts + 1) * 128],
                            wo_t[g][:, (h_i % 4) * 512:(h_i % 4 + 1) * 512],
                            start=(h_i == 0), stop=(h_i == NH - 1)),
                            waits=w, sem="pW" if h_i % 4 == 3 else None)
                        if ts == 3 and h_i % 4 == 3:
                            wo_r[g].frees.append(mm)
                    oT_s.frees.append(mm)
                    si = (db * 4 + ts) % 4
                    sst = sst_r[si]
                    plan.op("ve", (lambda h: h.nop(cycle_cnt=DELAY)) if DELAY else
                            (lambda h: h.tensor_copy(fen_v[:], ones1f[0:1, 0:4])),
                            waits=[mm], sem="vS")
                    t_sv = plan.op("ve", lambda h, i=si, xi=si, bank=bank: h.tensor_tensor(
                        s_st[i][:], pbank[bank][:], xo_t[xi][:], ALU.add),
                        waits=[mm, t_xo] + sst.take(), sem="vS")
                    psf[bank] = [t_sv]
                    xos.frees.append(t_sv)
                    t_sve = plan.op("ve", lambda h: h.tensor_copy(
                        fen_v[:], ones1f[0:1, 0:4]), sem="vS")
                    t_ss = plan.dma("gp", s_scr[qb * TB + ts * 128:qb * TB + (ts + 1) * 128,
                                               db * 512:(db + 1) * 512],
                                    s_st[si][:], waits=[t_sve])
                    if debug >= 2:
                        t_ss = plan.dma("gp",
                                        dbg["dbg_s"][qb * TB + ts * 128:qb * TB + (ts + 1) * 128,
                                                     db * 512:(db + 1) * 512],
                                        s_st[si][:], waits=[t_sve])
                    sst.frees.append(t_ss)
                    sstore_tok[(qb, ts, db)] = t_ss
                    t_ss_last_box[0] = t_ss

        t_end_pe = mm
        t_end_ve = plan.op("ve", lambda h: h.tensor_copy(
            fen_v[:], ones1f[0:1, 0:4]), waits=[t_end_pe], sem="vS")
        t_end_sc = plan.op("sc", lambda h: h.copy(
            fen_s[:], ones1f[0:1, 0:4]), waits=[t_end_pe], sem="sE")
        for i in range(8):
            psf[i] = [t_end_pe, t_end_ve, t_end_sc]
        plan.end_variant()
        return psf, [t_end_pe, t_end_ve, t_end_sc]

    psf_a, G2 = plan_p2(0)
    plan_p2(1)
    if debug >= 2:
        plan.dma("gp", dbg["dbg_oT"], oT[:], waits=G2[:3])
    psfree = psf_a
    G2 = G2 + [t_ss_last_box[0]]
    ctx2.close()
    ctx12.close()
    if not P3_ON:
        ctx0 if True else None
        meta["skip_p3"] = True
        return meta

    ctx3 = ExitStack()
    sb3 = mk_sb(ctx3)
    wg_t = [sb3(f"wg_t{i}", [128, D]) for i in range(3)]
    wu_t = [sb3(f"wu_t{i}", [128, D]) for i in range(3)]
    gs_t = [sb3(f"gs_t{i}", [128, 512], F32) for i in range(2)]
    h_g = sb3("h_g", [128, 4 * 512], F32)
    a_g = sb3("a_g", [128, 4 * 512], F32)
    b_g = sb3("b_g", [128, 6 * 512])
    w_g = sb3("w_g", [128, 2 * 512])
    m_g = sb3("m_g", [128, 4 * 512])
    hT_slab = sb3("hT_slab", [128, NIC * TB], side="right")
    wd_t = [sb3(f"wd_t{i}", [128, 1024]) for i in range(3)]
    sld = [sb3(f"sld{i}", [128, 512]) for i in range(2)]
    y_st = [sb3(f"y_st{i}", [128, 512], F32) for i in range(2)]

    wg_r = ring(3, G2)
    wu_r = ring(3, G2)
    gs_r = ring(2, G2)
    wd_r = ring(3, G2)
    sld_r = ring(2, G2)
    yst_r = ring(2, G2)
    slab_free = list(G2)
    for ob in range(NOB):
        grp_toks = []
        for ic in range(NIC):
            ii = ic % 2
            iw = ic % 3
            t_wg = plan.dma("gp", wg_t[iw][:], wgb_d[ic],
                            waits=wg_r[iw].take() + ([conv_g[ic]] if ob == 0 else []))
            t_wu = plan.dma("gp", wu_t[iw][:], wub_d[ic],
                            waits=wu_r[iw].take() + ([conv_u[ic]] if ob == 0 else []))
            bank_g = ic % 2
            bank_u = 2 + ic % 2
            mmg = mmu = None
            for c in range(NCH):
                mmg = plan.op("pe", lambda h, c=c, iw=iw, ob=ob, bank=bank_g: h.matmul(
                    pbank[bank][:], wg_t[iw][:, c * 128:(c + 1) * 128],
                    xnT_own[:, (ob * NCH + c) * TB:(ob * NCH + c + 1) * TB],
                    start=(c == 0), stop=(c == NCH - 1)),
                    waits=[t_wg] + (psfree[bank_g] if c == 0 else []))
            wg_r[iw].frees.append(mmg)
            for c in range(NCH):
                mmu = plan.op("pe", lambda h, c=c, iw=iw, ob=ob, bank=bank_u: h.matmul(
                    pbank[bank][:], wu_t[iw][:, c * 128:(c + 1) * 128],
                    xnT_own[:, (ob * NCH + c) * TB:(ob * NCH + c + 1) * TB],
                    start=(c == 0), stop=(c == NCH - 1)),
                    waits=[t_wu] + (psfree[bank_u] if c == 0 else []))
            wu_r[iw].frees.append(mmu)
            gss = gs_r[ii]
            t_gs = plan.op("sc", lambda h, ii=ii, bank=bank_g: h.activation(
                gs_t[ii][:], pbank[bank][:], AF.Silu), waits=[mmg] + gss.take())
            psfree[bank_g] = [t_gs]
            plan.op("ve", (lambda h: h.nop(cycle_cnt=DELAY_S)) if DELAY else
                    (lambda h: h.tensor_copy(fen_v[:], ones1f[0:1, 0:4])),
                    waits=[mmu])
            t_h = plan.op("ve", lambda h, ic=ic, ii=ii, bank=bank_u: h.tensor_tensor(
                h_g[:, (ic % 4) * 512:(ic % 4 + 1) * 512], gs_t[ii][:],
                pbank[bank][:], ALU.mult), waits=[mmu, t_gs])
            psfree[bank_u] = [t_h]
            gss.frees.append(t_h)

            if ic % 4 == 3:
                t_abs = None
                for j in range(4):
                    t_abs = plan.op("sc", lambda h, j=j: h.activation(
                        a_g[:, j * 512:(j + 1) * 512], h_g[:, j * 512:(j + 1) * 512],
                        AF.Abs),
                        waits=[t_h] + (slab_free if (j == 0 and ic == 3) else []))
                prs = [(0, 1), (0, 2), (0, 3), (1, 2), (1, 3), (2, 3)]
                for k, (i1, i2) in enumerate(prs):
                    plan.op("ve", lambda h, k=k, i1=i1, i2=i2: h.tensor_tensor(
                        b_g[:, k * 512:(k + 1) * 512], a_g[:, i1 * 512:(i1 + 1) * 512],
                        a_g[:, i2 * 512:(i2 + 1) * 512], ALU.is_ge),
                        waits=([t_abs] if k == 0 else []))
                Bx = lambda k: b_g[:, k * 512:(k + 1) * 512]
                W0 = w_g[:, 0:512]
                W1 = w_g[:, 512:1024]
                Mx = lambda j: m_g[:, j * 512:(j + 1) * 512]
                plan.op("ve", lambda h: h.tensor_tensor(W0, Bx(0), Bx(1), ALU.add))
                plan.op("ve", lambda h: h.tensor_tensor(W0, W0, Bx(2), ALU.add))
                plan.op("ve", lambda h: h.tensor_single_scalar(Mx(0), W0, 1.5, ALU.is_ge))
                plan.op("ve", lambda h: h.tensor_tensor(W0, Bx(3), Bx(4), ALU.add))
                plan.op("ve", lambda h: h.tensor_tensor(W0, W0, Bx(0), ALU.subtract))
                plan.op("ve", lambda h: h.tensor_single_scalar(Mx(1), W0, 0.5, ALU.is_ge))
                plan.op("ve", lambda h: h.tensor_tensor(W0, Bx(1), Bx(3), ALU.add))
                plan.op("ve", lambda h: h.tensor_tensor(W0, Bx(5), W0, ALU.subtract))
                plan.op("ve", lambda h: h.tensor_single_scalar(Mx(2), W0, -0.5, ALU.is_ge))
                plan.op("ve", lambda h: h.tensor_tensor(W1, Bx(2), Bx(4), ALU.add))
                plan.op("ve", lambda h: h.tensor_tensor(W1, W1, Bx(5), ALU.add))
                plan.op("ve", lambda h: h.tensor_single_scalar(Mx(3), W1, 1.5, ALU.is_le))
                tk = None
                for j in range(4):
                    icj = ic - 3 + j
                    tk = plan.op("ve", lambda h, j=j, icj=icj: h.tensor_tensor(
                        hT_slab[:, icj * TB:(icj + 1) * TB],
                        h_g[:, j * 512:(j + 1) * 512], Mx(j), ALU.mult))
                grp_toks.append(tk)

        for dbp in range(2):
            last_mm = [None] * 8
            for ic in range(NIC):
                iw = ic % 3
                t_wd = plan.dma("gp", wd_t[iw][:],
                                wdb_d[ic][:, dbp * 1024:(dbp + 1) * 1024],
                                waits=wd_r[iw].take() +
                                ([conv_d_toks[ic]] if ob == 0 and dbp == 0 else []))
                gtok = grp_toks[ic // 4]
                for db2 in range(2):
                    for ts in range(4):
                        bank = ts + 4 * db2
                        w = [t_wd, gtok]
                        if ic == 0:
                            w = w + psfree[bank]
                        last_mm[bank] = plan.op(
                            "pe",
                            lambda h, ic=ic, iw=iw, ts=ts, db2=db2, bank=bank: h.matmul(
                                pbank[bank][:],
                                hT_slab[:, ic * TB + ts * 128:ic * TB + (ts + 1) * 128],
                                wd_t[iw][:, db2 * 512:(db2 + 1) * 512],
                                start=(ic == 0), stop=(ic == NIC - 1)),
                            waits=w, sem="pD" if ic == NIC - 1 else None)
                wd_r[iw].frees.append(last_mm[7])
            for db2 in range(2):
                for ts in range(4):
                    bank = ts + 4 * db2
                    db = dbp * 2 + db2
                    si = (db2 * 4 + ts) % 2
                    t_sl = plan.dma("sp", sld[si][:],
                                    s_scr[ob * TB + ts * 128:ob * TB + (ts + 1) * 128,
                                          db * 512:(db + 1) * 512],
                                    waits=sld_r[si].take() + [sstore_tok[(ob, ts, db)]])
                    t_y = plan.op("ve", lambda h, i=si, bank=bank: h.tensor_tensor(
                        y_st[i][:], pbank[bank][:], sld[i][:], ALU.add),
                        waits=[last_mm[bank], t_sl] + yst_r[si].take())
                    psfree[bank] = [t_y]
                    sld_r[si].frees.append(t_y)
                    t_yd = plan.dma("gp", y_d[ob * TB + ts * 128:ob * TB + (ts + 1) * 128,
                                              db * 512:(db + 1) * 512],
                                    y_st[si][:], waits=[t_y])
                    yst_r[si].frees.append(t_yd)
        slab_free = [last_mm[7]]
    ctx3.close()
    return meta


def finish(meta):
    nc, plan = meta["nc"], meta["plan"]
    plan.resolve()
    with meta["ctx0"]:
        with ExitStack() as sctx:
            sems = {name: sctx.enter_context(nc.semaphore(name)) for name in ALLSEMS}
            lastw = {}

            def emit_ops(h, eng, ops):
                for waits, emit, tok, dma in ops:
                    need = {}
                    for w in waits:
                        if w.eng == eng and eng != "gp":
                            continue
                        assert w.sem is not None and w.val is not None, \
                            ("unresolved cross wait", eng, w.eng, w.key)
                        need[w.sem] = max(need.get(w.sem, 0), w.val)
                    for sname in sorted(need):
                        val = need[sname]
                        assert val <= plan.final[sname], \
                            ("wait exceeds final -> hang", eng, sname, val,
                             plan.final[sname])
                        key = (eng, sname)
                        if lastw.get(key, 0) >= val:
                            continue
                        h.wait_ge(sems[sname], val)
                        lastw[key] = val
                    inst = emit(h)
                    if tok.sem is not None and tok.val is not None:
                        if eng == "gp" and not dma:
                            inst = h.drain()
                        inst.then_inc(sems[tok.sem], 16 if dma else 1)

            cuts = plan.cuts
            with nc.Block() as b1:
                @b1.sync
                def _(h):
                    emit_ops(h, "sp", plan.lists["sp"][:cuts["sp"]])

                @b1.gpsimd
                def _(h):
                    emit_ops(h, "gp", plan.lists["gp"][:cuts["gp"]])

                @b1.tensor
                def _(h):
                    emit_ops(h, "pe", plan.lists["pe"][:cuts["pe"]])

                @b1.vector
                def _(h):
                    emit_ops(h, "ve", plan.lists["ve"][:cuts["ve"]])

                @b1.scalar
                def _(h):
                    emit_ops(h, "sc", plan.lists["sc"][:cuts["sc"]])

            if not meta["stop_after_p1"]:
                pidx = [nc.tensor.partition_id() & 1,
                        nc.vector.partition_id() & 1,
                        nc.scalar.partition_id() & 1]
                saved = dict(lastw)
                for v in nc.Switch(engines=[nc.tensor, nc.vector, nc.scalar],
                                   index=pidx, n=2):
                    lastw.clear()
                    lastw.update(saved)
                    emit_ops(nc.tensor, "pe", plan.var[v]["pe"])
                    emit_ops(nc.vector, "ve", plan.var[v]["ve"])
                    emit_ops(nc.scalar, "sc", plan.var[v]["sc"])
                lastw.clear()
                lastw.update(saved)

            with nc.Block() as b2:
                @b2.sync
                def _(h):
                    emit_ops(h, "sp", plan.lists["sp"][cuts["sp"]:])

                @b2.tensor
                def _(h):
                    emit_ops(h, "pe", plan.lists["pe"][cuts["pe"]:])

                @b2.vector
                def _(h):
                    emit_ops(h, "ve", plan.lists["ve"][cuts["ve"]:])

                @b2.scalar
                def _(h):
                    emit_ops(h, "sc", plan.lists["sc"][cuts["sc"]:])

                @b2.gpsimd
                def _(h):
                    emit_ops(h, "gp", plan.lists["gp"][cuts["gp"]:])
                    h.wait_ge(sems["gL"], plan.final["gL"])
    return nc


def ffn_perm():
    n = np.arange(I_DIM)
    return 4 * ((n // 512) * 128 + (n % 128)) + (n // 128) % 4


def make_in_maps(inp):
    x = np.asarray(inp["L0_attn_x"])
    Wq = np.asarray(inp["L0_attn_Wq"])
    Wk = np.asarray(inp["L0_attn_Wk"])
    Wv = np.asarray(inp["L0_attn_Wv"])
    Wo = np.asarray(inp["L0_attn_Wo"])
    bq = np.ascontiguousarray(np.asarray(inp["L0_attn_bq"]).reshape(1, D))
    bk = np.ascontiguousarray(np.asarray(inp["L0_attn_bk"]).reshape(1, 512))
    bv = np.ascontiguousarray(np.asarray(inp["L0_attn_bv"]).reshape(1, 512))
    perm = ffn_perm()
    Wg = np.asarray(inp["L0_ffn_w_gate"])[:, perm]
    Wu = np.asarray(inp["L0_ffn_w_up"])[:, perm]
    Wd = np.asarray(inp["L0_ffn_w_down"])[perm, :]

    wq_h = np.ascontiguousarray(
        Wq.reshape(NCH, 128, NH, 128).transpose(2, 1, 0, 3).reshape(NH, 128, D))
    wk_h = np.ascontiguousarray(Wk.reshape(NCH, 128, 512).transpose(1, 0, 2))
    wv_h = np.ascontiguousarray(Wv.reshape(NCH, 128, 512).transpose(1, 0, 2))
    wo_h = np.ascontiguousarray(
        Wo.reshape(4, 4, 128, 4, 512).transpose(3, 0, 2, 1, 4).reshape(4, 4, 128, D))
    wg_h = np.ascontiguousarray(
        Wg.reshape(NCH, 128, NIC, 128).transpose(2, 1, 0, 3).reshape(NIC, 128, D))
    wu_h = np.ascontiguousarray(
        Wu.reshape(NCH, 128, NIC, 128).transpose(2, 1, 0, 3).reshape(NIC, 128, D))
    wd_h = np.ascontiguousarray(Wd.reshape(NIC, 128, D))

    in_maps = []
    for c in range(8):
        b, half = c // 2, c % 2
        own = Q0S[half]
        in_maps.append(dict(
            x_seq=np.ascontiguousarray(x[b]),
            x_own=np.ascontiguousarray(
                np.concatenate([x[b, s:s + TB] for s in own], axis=0)),
            wq=wq_h, wk=wk_h, wv=wv_h, wo=wo_h, bq=bq, bk=bk, bv=bv,
            wg=wg_h, wu=wu_h, wd=wd_h))
    return in_maps


def assemble(results):
    out = np.zeros((4, T, D), dtype=np.float32)
    for c in range(8):
        b, half = c // 2, c % 2
        y = np.asarray(results[c]["y"])
        for i, s in enumerate(Q0S[half]):
            out[b, s:s + TB] = y[i * TB:(i + 1) * TB]
    return out


_CACHE = {}


def kernel(**inputs):
    from concourse.bass_utils import run_bass_kernel_spmd
    in_maps = make_in_maps(inputs)
    if "nc" not in _CACHE:
        meta = build(debug=0)
        _CACHE["nc"] = finish(meta)
    res = run_bass_kernel_spmd(_CACHE["nc"], in_maps, core_ids=list(range(8)))
    return assemble(res.results)


# version 17
# speedup vs baseline: 1.1479x; 1.1479x over previous
import numpy as np
import ml_dtypes
import concourse.bass as bass
import concourse.mybir as mybir
from contextlib import ExitStack

F32 = mybir.dt.float32
BF16 = mybir.dt.bfloat16
I32 = mybir.dt.int32
AF = mybir.ActivationFunctionType
ALU = mybir.AluOpType
BF = ml_dtypes.bfloat16
PI = float(np.pi)
TWO_PI_HI = float(np.float32(2 * np.pi))
TWO_PI_LO = float(2 * np.pi - np.float64(np.float32(2 * np.pi)))
LN1E6 = float(np.log(1e6))
SCALE = float(128 ** -0.5)

D = 2048
NCH = 16
T = 4096
TB = 512
NB = T // TB
NOB = 4
NH, NKV = 16, 4
I_DIM = 8192
NIC = 64
Q0S = [[0, 512, 3072, 3584], [1024, 1536, 2048, 2560]]
P1_BLOCKS = NB + NOB
P1_SUB = 9
P1_FINE = 9
P0_CLOSE = True
P3_ON = True
NQB = NOB
DELAY = 1000
DELAY_S = 1000

ENGS = ("pe", "ve", "sc", "sp", "gp")
DEFSEM = {"pe": "pC", "ve": "vC", "sc": "sC", "gp": "gC", "sp": "sL"}
SPECIAL = ("pQ", "pW", "vS", "pD", "sE")
ALLSEMS = ("pC", "vC", "sC", "gC", "sL", "gL", "cL") + SPECIAL
DMASEMS = ("sL", "gL", "cL")


class Tok:
    __slots__ = ("eng", "key", "sem", "val")

    def __init__(self, eng, key, sem=None):
        self.eng = eng
        self.key = key
        self.sem = sem
        self.val = None


class Plan:
    def __init__(self):
        self.lists = {e: [] for e in ENGS}
        self.var = [{e: [] for e in ("pe", "ve", "sc")} for _ in range(2)]
        self.mode = None
        self.replay = None
        self.record = None
        self.cuts = None

    def op(self, eng, emit, waits=(), sem=None, dma=False):
        waits = tuple(w for w in waits if w is not None)
        if self.mode == 1 and eng in ("sp", "gp"):
            return self.replay.pop(0)
        if self.mode is not None and eng in ("pe", "ve", "sc"):
            lname, lst = f"v{self.mode}", self.var[self.mode][eng]
        else:
            lname, lst = "main", self.lists[eng]
        if dma:
            sem = {"gp": "gL", "sc": "cL"}.get(eng, "sL")
        tok = Tok(eng, (lname, eng, len(lst)), sem=sem)
        lst.append([waits, emit, tok, dma])
        if self.mode == 0 and eng in ("sp", "gp"):
            self.record.append(tok)
        return tok

    def dma(self, eng, out, in_, waits=()):
        return self.op(eng, lambda h, o=out, i=in_: h.dma_start(out=o, in_=i),
                       waits=waits, dma=True)

    def tdma(self, out, in_, waits=(), eng="sp"):
        return self.op(eng, lambda h, o=out, i=in_: h.dma_start_transpose(out=o, in_=i),
                       waits=waits, dma=True)

    def begin_variant(self, v):
        self.mode = v
        if v == 0:
            self.record = []
        else:
            self.replay = list(self.record)

    def end_variant(self):
        self.mode = None

    def resolve(self):
        used = set()
        groups = [(e, self.lists[e]) for e in ENGS]
        for v in range(2):
            groups += [(e, self.var[v][e]) for e in ("pe", "ve", "sc")]
        for e, ops in groups:
            for waits, emit, tok, dma in ops:
                for w in waits:
                    if w.eng != e or e == "gp" or dma:
                        used.add(w.key)
        cnt = {s: 0 for s in ALLSEMS}

        def assign(eng, ops):
            for waits, emit, tok, dma in ops:
                if dma or tok.sem is not None:
                    sem = tok.sem or DEFSEM[eng]
                    tok.sem = sem
                    cnt[sem] += 16 if dma else 1
                    tok.val = cnt[sem]
                elif tok.key in used:
                    tok.sem = DEFSEM[eng]
                    cnt[tok.sem] += 1
                    tok.val = cnt[tok.sem]

        for e in ("sp", "gp"):
            assign(e, self.lists[e])
        for e in ("pe", "ve", "sc"):
            cut = self.cuts[e]
            assign(e, self.lists[e][:cut])
            snap = dict(cnt)
            assign(e, self.var[0][e])
            cnt0 = dict(cnt)
            for s in ALLSEMS:
                cnt[s] = snap[s]
            assign(e, self.var[1][e])
            for s in ALLSEMS:
                assert cnt[s] == cnt0[s], ("variant sem mismatch", e, s, cnt[s], cnt0[s])
            assign(e, self.lists[e][cut:])
        self.final = cnt


class Slot:
    def __init__(self, init=()):
        self.frees = list(init)

    def take(self):
        f = self.frees
        self.frees = []
        return f


def ring(n, init=()):
    return [Slot(init) for _ in range(n)]


def build(debug=0):
    nc = bass.Bass("TRN2", target_bir_lowering=False, debug=False)
    plan = Plan()
    ctx0 = ExitStack()

    def mk_sb(stack):
        def f(name, shape, dt=BF16, side=None):
            kw = {"side": side} if side else {}
            return stack.enter_context(nc.sbuf_tensor(name, shape, dt, **kw))
        return f

    sb0 = mk_sb(ctx0)
    psr = lambda name, shape, dt=F32: ctx0.enter_context(nc.psum_tensor(name, shape, dt))

    x_seq = nc.dram_tensor("x_seq", [T, D], BF16, kind="ExternalInput").ap()
    x_own = nc.dram_tensor("x_own", [NOB * TB, D], BF16, kind="ExternalInput").ap()
    wq_d = nc.dram_tensor("wq", [NH, 128, D], BF16, kind="ExternalInput").ap()
    wk_d = nc.dram_tensor("wk", [128, NCH, 512], BF16, kind="ExternalInput").ap()
    wv_d = nc.dram_tensor("wv", [128, NCH, 512], BF16, kind="ExternalInput").ap()
    wo_d = nc.dram_tensor("wo", [4, 4, 128, D], BF16, kind="ExternalInput").ap()
    bq_d = nc.dram_tensor("bq", [1, D], BF16, kind="ExternalInput").ap()
    bk_d = nc.dram_tensor("bk", [1, 512], BF16, kind="ExternalInput").ap()
    bv_d = nc.dram_tensor("bv", [1, 512], BF16, kind="ExternalInput").ap()
    wg_d = nc.dram_tensor("wg", [NIC, 128, D], F32, kind="ExternalInput").ap()
    wu_d = nc.dram_tensor("wu", [NIC, 128, D], F32, kind="ExternalInput").ap()
    wd_d = nc.dram_tensor("wd", [NIC, 128, D], F32, kind="ExternalInput").ap()
    y_d = nc.dram_tensor("y", [NOB * TB, D], F32, kind="ExternalOutput").ap()
    s_scr = nc.dram_tensor("s_scr", [NOB * TB, D], BF16, kind="Internal").ap()
    wgb_d = nc.dram_tensor("wgb", [NIC, 128, D], BF16, kind="Internal").ap()
    wub_d = nc.dram_tensor("wub", [NIC, 128, D], BF16, kind="Internal").ap()
    wdb_d = nc.dram_tensor("wdb", [NIC, 128, D], BF16, kind="Internal").ap()

    dbg = {}
    if debug >= 1:
        for nm, shape in (("dbg_xnT", [128, NOB * NCH * TB]), ("dbg_kT", [128, NKV * T]),
                          ("dbg_v", [128, (T // 128) * 512]), ("dbg_cos", [128, T]),
                          ("dbg_sin", [128, T])):
            dbg[nm] = nc.dram_tensor(nm, shape, BF16, kind="ExternalOutput").ap()
    if debug >= 2:
        dbg["dbg_s"] = nc.dram_tensor("dbg_s", [NOB * TB, D], BF16,
                                      kind="ExternalOutput").ap()

    ones_bf = sb0("ones_bf", [128, 1])
    fen_v = sb0("fen_v", [1, 4], F32)
    fen_s = sb0("fen_s", [1, 4], F32)
    ones1f = sb0("ones1f", [1, 128], F32)
    ones1b = sb0("ones1b", [1, 128])
    invf = sb0("invf", [128, 1], F32)
    epsb = sb0("epsb", [128, 1], F32)
    m0 = sb0("m0", [128, 896])
    bq_c = sb0("bq_c", [128, NH], F32)
    bk_c = sb0("bk_c", [128, NKV], F32)
    bv_c = sb0("bv_c", [128, NKV], F32)
    kraw = [sb0(f"kraw{i}", [128, TB]) for i in range(2)]
    rope_r = sb0("rope_r", [128, TB])
    rope_a = sb0("rope_a", [128, TB])
    rope_b = sb0("rope_b", [128, TB])
    xnT_own = sb0("xnT_own", [128, NOB * NCH * TB])

    ctx12 = ExitStack()
    sb12 = mk_sb(ctx12)
    cosT = sb12("cosT", [128, T], side="right")
    sinS = sb12("sinS", [128, T], side="right")
    kT = sb12("kT", [128, NKV * T], side="right")
    v_sb = sb12("v_sb", [128, (T // 128) * 512], side="right")

    pball = psr("pball", [128, 4096])
    pbank = [pball[:, i * 512:(i + 1) * 512] for i in range(8)]
    psfree = {i: [] for i in range(8)}

    ctxp0 = ExitStack()
    sbp = mk_sb(ctxp0)
    browb = sbp("browb", [1, D])
    frow = sbp("frow", [1, 128], F32)
    tsc = [sbp(f"tsc{i}", [128, 512], F32) for i in range(3)]
    tsi = sbp("tsi", [128, 512], I32)
    trow = sbp("trow", [1, T], F32)

    t_on = plan.op("gp", lambda h: h.memset(ones_bf[:], 1.0), sem="gC")
    t_eps = plan.op("gp", lambda h: h.memset(epsb[:], 1e-5), sem="gC")
    t_o1f = plan.op("gp", lambda h: h.memset(ones1f[:], 1.0), sem="gC")
    t_on2 = plan.op("gp", lambda h: h.memset(ones1b[:], 1.0), sem="gC")
    t_m0a = plan.op("gp", lambda h: h.memset(m0[:], 1.0), sem="gC")
    t_m0 = plan.op("gp", lambda h: h.affine_select(
        m0[:], m0[:], [[1, 896]], ALU.is_ge, 0.0, base=-384, channel_multiplier=-1),
        waits=[t_m0a], sem="gC")
    t_fr = plan.op("gp", lambda h: h.iota(
        frow[:].rearrange("a (b c) -> a b c", c=64), [[0, 2], [1, 64]], base=0,
        channel_multiplier=0, allow_small_or_imprecise_dtypes=True), sem="gC")
    t_tr = plan.op("gp", lambda h: h.iota(
        trow[:], [[1, T]], base=0, channel_multiplier=0,
        allow_small_or_imprecise_dtypes=True), sem="gC")
    t_ifr = plan.op("sc", lambda h: h.activation(
        frow[:], frow[:], AF.Exp, scale=-LN1E6 / 64.0), waits=[t_fr])
    t_bcf = plan.op("pe", lambda h: h.matmul(
        pbank[0][:, 0:1], frow[:], ones1f[0:1, 0:1], start=True, stop=True),
        waits=[t_ifr, t_o1f])
    plan.op("ve", (lambda h: h.nop(cycle_cnt=DELAY)) if DELAY else
            (lambda h: h.tensor_copy(fen_v[:], ones1f[0:1, 0:4])), waits=[t_bcf])
    t_if = plan.op("ve", lambda h: h.tensor_copy(invf[:], pbank[0][:, 0:1]),
                   waits=[t_bcf])

    cos_last = sin_last = None
    t_an_prev = None
    for b in range(T // 512):
        sl = slice(b * 512, (b + 1) * 512)
        t_bc = plan.op("pe", lambda h, sl=sl: h.matmul(
            pbank[0][:], ones1f[:], trow[0:1, sl], start=True, stop=True),
            waits=[t_tr, t_if, t_an_prev, t_o1f])
        plan.op("ve", (lambda h: h.nop(cycle_cnt=DELAY)) if DELAY else
                (lambda h: h.tensor_copy(fen_v[:], ones1f[0:1, 0:4])), waits=[t_bc])
        t_an = plan.op("ve", lambda h: h.tensor_scalar(
            tsc[0][:], pbank[0][:], invf[:], None, ALU.mult), waits=[t_bc])
        t_an_prev = t_an
        for table in (cosT, sinS):
            off = PI / 2 if table is cosT else 0.0
            plan.op("ve", lambda h, off=off: h.tensor_scalar(
                tsc[1][:], tsc[0][:], off, 1.0 / (2 * PI), ALU.add, ALU.mult))
            plan.op("ve", lambda h: h.tensor_copy(tsi[:], tsc[1][:]))
            plan.op("ve", lambda h: h.tensor_copy(tsc[1][:], tsi[:]))
            plan.op("ve", lambda h, off=off: h.tensor_single_scalar(
                tsc[2][:], tsc[0][:], off, ALU.add))
            plan.op("ve", lambda h: h.scalar_tensor_tensor(
                tsc[2][:], tsc[1][:], -TWO_PI_HI, tsc[2][:], ALU.mult, ALU.add))
            plan.op("ve", lambda h: h.scalar_tensor_tensor(
                tsc[2][:], tsc[1][:], -TWO_PI_LO, tsc[2][:], ALU.mult, ALU.add))
            plan.op("ve", lambda h: h.tensor_single_scalar(
                tsc[1][:], tsc[2][:], PI, ALU.is_gt))
            t_r3 = plan.op("ve", lambda h: h.scalar_tensor_tensor(
                tsc[2][:], tsc[1][:], -TWO_PI_HI, tsc[2][:], ALU.mult, ALU.add))
            if table is cosT:
                t_s = plan.op("sc", lambda h, sl=sl: h.activation(
                    cosT[:, sl], tsc[2][:], AF.Sin), waits=[t_r3])
                cos_last = t_s
                plan.op("ve", lambda h: h.engine_nop(), waits=[t_s])
            else:
                t_ng = plan.op("ve", lambda h: h.tensor_single_scalar(
                    tsc[1][0:64, :], tsc[2][0:64, :], -1.0, ALU.mult))
                t_s1 = plan.op("sc", lambda h, sl=sl: h.activation(
                    sinS[0:64, sl], tsc[1][0:64, :], AF.Sin), waits=[t_ng])
                t_s2 = plan.op("sc", lambda h, sl=sl: h.activation(
                    sinS[64:128, sl], tsc[2][64:128, :], AF.Sin), waits=[t_r3])
                sin_last = t_s2
                plan.op("ve", lambda h: h.engine_nop(), waits=[t_s1, t_s2])
    psfree[0] = [t_an_prev]
    t_cos, t_sin = cos_last, sin_last

    def bias_chunks(bias_d, n_ch, dst, bank, prev=None):
        t_l = plan.dma("sp", browb[0:1, 0:n_ch * 128], bias_d[0:1, :],
                       waits=[prev] if prev else [])
        mm = None
        for c in range(n_ch):
            mm = plan.op("pe", lambda h, c=c, bank=bank, n=n_ch: h.matmul(
                pbank[bank][:, c:c + 1], browb[0:1, c * 128:(c + 1) * 128],
                ones1b[0:1, 0:1], start=(c == 0), stop=(c == n_ch - 1)),
                waits=[t_l, t_on2] + (psfree[bank] if c == 0 else []))
        plan.op("sc", (lambda h: h.nop(cycle_cnt=DELAY)) if DELAY else
                (lambda h: h.copy(fen_s[:], ones1f[0:1, 0:4])), waits=[mm])
        return plan.op("sc", lambda h, dst=dst, bank=bank, n=n_ch: h.copy(
            dst[:], pbank[bank][:, 0:n_ch]), waits=[mm]), mm

    t_bq, mm_q = bias_chunks(bq_d, NH, bq_c, 1)
    t_bk, mm_k = bias_chunks(bk_d, NKV, bk_c, 2, prev=mm_q)
    t_bv, t_bias_mm_last = bias_chunks(bv_d, NKV, bv_c, 3, prev=mm_k)

    conv_g = [None] * NIC
    conv_u = [None] * NIC
    conv_d_toks = []
    psfree[1], psfree[2], psfree[3] = [t_bq], [t_bk], [t_bv]
    if P0_CLOSE:
        ctxp0.close()

    ctx1 = ExitStack()
    sb1 = mk_sb(ctx1)
    xTf = [sb1(f"xTf{i}", [128, TB]) for i in range(4)]
    sqt = [sb1(f"sqt{i}", [128, TB]) for i in range(2)]
    st_mean = sb1("st_mean", [1, TB], F32)
    st_msq = sb1("st_msq", [1, TB], F32)
    st_var = sb1("st_var", [1, TB], F32)
    st_rstd = sb1("st_rstd", [1, TB], F32)
    st_b = sb1("st_b", [1, TB], F32)
    ab_sb = sb1("ab_sb", [128, 2 * TB])
    xnc = [sb1(f"xnc{i}", [128, TB]) for i in range(2)]
    wk_sb = sb1("wk_sb", [128, NCH * 512])
    wv_sb = sb1("wv_sb", [128, NCH * 512])

    t_p0_pe = t_bias_mm_last
    t_p0_ve = plan.op("ve", lambda h: h.tensor_copy(fen_v[:], ones1f[0:1, 0:4]),
                      sem="vC")
    t_p0_sc = plan.op("sc", lambda h: h.copy(fen_s[:], ones1f[0:1, 0:4]),
                      sem="sC")
    G0 = [t_p0_pe, t_p0_ve, t_p0_sc]

    t_wk = plan.dma("sp", wk_sb[:].rearrange("p (c n) -> p c n", n=512), wk_d[:],
                    waits=G0)
    t_wv = plan.dma("sp", wv_sb[:].rearrange("p (c n) -> p c n", n=512), wv_d[:])

    last_pe_p1 = t_bias_mm_last
    xTf_r = ring(4, G0)
    xnc_r = ring(2)
    sq_r = ring(2, G0)
    kraw_r = ring(2)
    ab_s = Slot(G0)

    for blk in range(min(NB + NOB, P1_BLOCKS)):
        own = blk >= NB
        t0 = (blk - NB) * TB if own else blk * TB
        src = x_own if own else x_seq
        mm_sum = sq_mm = None
        for c in range(NCH):
            sx = xTf_r[c % 4]
            t_x = plan.tdma(xTf[c % 4][:],
                            src[t0:t0 + TB, c * 128:(c + 1) * 128], waits=sx.take())
            mm_sum = plan.op("pe", lambda h, c=c, i=c % 4: h.matmul(
                pbank[0][0:1, :], ones_bf[:], xTf[i][:],
                start=(c == 0), stop=(c == NCH - 1)),
                waits=[t_x, t_on] + (psfree[0] if c == 0 else []))
            sq = sq_r[c % 2]
            t_sq = plan.op("sc", lambda h, i=c % 4, j=c % 2: h.activation(
                sqt[j][:], xTf[i][:], AF.Square), waits=[t_x] + sq.take())
            sq_mm = plan.op("pe", lambda h, c=c, i=c % 2: h.matmul(
                pbank[1][0:1, :], ones_bf[:], sqt[i][:],
                start=(c == 0), stop=(c == NCH - 1)),
                waits=[t_sq] + (psfree[1] if c == 0 else []))
            sq.frees.append(sq_mm)
            sx.frees += [mm_sum, t_sq]
        if P1_SUB < 2:
            psfree[0] = [plan.op("ve", lambda h: h.engine_nop(), waits=[mm_sum])]
            psfree[1] = [plan.op("ve", lambda h: h.engine_nop(), waits=[sq_mm])]
            continue
        last_pe_p1 = sq_mm
        plan.op("ve", (lambda h: h.nop(cycle_cnt=DELAY)) if DELAY else
                (lambda h: h.tensor_copy(fen_v[:], ones1f[0:1, 0:4])),
                waits=[mm_sum, sq_mm])
        t_mean = plan.op("ve", lambda h: h.tensor_single_scalar(
            st_mean[:], pbank[0][0:1, :], 1.0 / D, ALU.mult), waits=[mm_sum])
        if P1_FINE < 2:
            psfree[0] = [t_mean]
            psfree[1] = [plan.op("ve", lambda h: h.tensor_single_scalar(
                st_msq[:], pbank[1][0:1, :], 1.0 / D, ALU.mult), waits=[sq_mm])]
            continue
        t_msq = plan.op("ve", lambda h: h.tensor_single_scalar(
            st_msq[:], pbank[1][0:1, :], 1.0 / D, ALU.mult), waits=[sq_mm])
        psfree[0] = [t_mean]
        psfree[1] = [t_msq]
        plan.op("ve", lambda h: h.tensor_tensor(
            st_b[:], st_mean[:], st_mean[:], ALU.mult))
        t_var = plan.op("ve", lambda h: h.tensor_tensor(
            st_var[:], st_msq[:], st_b[:], ALU.subtract))
        if P1_FINE < 3:
            psfree[0] = [t_mean]
            psfree[1] = [t_msq]
            continue
        t_ln = plan.op("sc", lambda h: h.activation(
            st_var[:], st_var[:], AF.Ln, bias=epsb[0:1, :]), waits=[t_var, t_eps])
        plan.op("sc", lambda h: h.nop(cycle_cnt=DELAY_S))
        t_rstd = plan.op("sc", lambda h: h.activation(
            st_rstd[:], st_var[:], AF.Exp, scale=-0.5), waits=[t_ln])
        if P1_FINE < 4:
            psfree[0] = [t_mean]
            psfree[1] = [t_msq]
            continue
        t_b2 = plan.op("ve", lambda h: h.scalar_tensor_tensor(
            st_b[:], st_mean[:], -1.0, st_rstd[:], ALU.mult, ALU.mult),
            waits=[t_rstd])
        if P1_FINE < 5:
            psfree[0] = [t_mean]
            psfree[1] = [t_msq]
            continue
        plan.op("pe", lambda h: h.nop(cycle_cnt=DELAY), waits=[t_rstd, t_b2])
        t_bca = plan.op("pe", lambda h: h.matmul(
            pbank[2][:], ones1f[:], st_rstd[:], start=True, stop=True),
            waits=[t_rstd] + psfree[2])
        t_bcb = plan.op("pe", lambda h: h.matmul(
            pbank[3][:], ones1f[:], st_b[:], start=True, stop=True),
            waits=[t_b2] + psfree[3])
        if P1_FINE < 6:
            psfree[2] = [plan.op("ve", lambda h: h.engine_nop(), waits=[t_bca])]
            psfree[3] = [plan.op("ve", lambda h: h.engine_nop(), waits=[t_bcb])]
            if P1_FINE == 5.5:
                plan.op("sc", lambda h: h.copy(st_b[:], st_mean[:]))
                plan.op("sc", lambda h: h.copy(st_b[:], st_mean[:]))
            if P1_FINE == 5.7:
                plan.op("ve", lambda h: h.tensor_copy(ab_sb[:, 0:TB], pbank[2][:]),
                        waits=[t_bca] + ab_s.take())
                plan.op("ve", lambda h: h.tensor_copy(ab_sb[:, TB:2 * TB], pbank[3][:]),
                        waits=[t_bcb])
            if P1_FINE == 5.8:
                plan.op("sc", lambda h: h.copy(sqt[0][:], pbank[2][:]),
                        waits=[t_bca])
                plan.op("sc", lambda h: h.copy(sqt[1][:], pbank[3][:]),
                        waits=[t_bcb])
            if P1_FINE == 5.81:
                plan.op("sc", lambda h: h.copy(sqt[0][:], pbank[2][:]),
                        waits=[t_bca])
            if P1_FINE == 5.82:
                plan.op("sc", lambda h: h.copy(sqt[0][:], pbank[3][:]),
                        waits=[t_bcb])
            if P1_FINE == 5.83:
                plan.op("sc", lambda h: h.copy(sqt[0][:, 0:4], pbank[2][:, 0:4]),
                        waits=[t_bca])
            if P1_FINE == 5.84:
                plan.op("sc", lambda h: h.copy(sqt[0][:], pbank[2][:]),
                        waits=[t_bca, t_bcb])
            continue
        abfr = ab_s.take()
        plan.op("sc", (lambda h: h.nop(cycle_cnt=DELAY)) if DELAY else
                (lambda h: h.copy(fen_s[:], ones1f[0:1, 0:4])), waits=[t_bcb])
        t_ea = plan.op("sc", lambda h: h.copy(ab_sb[:, 0:TB], pbank[2][:]),
                       waits=[t_bca] + abfr)
        t_eb = plan.op("sc", lambda h: h.copy(ab_sb[:, TB:2 * TB], pbank[3][:]),
                       waits=[t_bcb])
        psfree[2], psfree[3] = [t_ea], [t_eb]

        if P1_SUB < 3:
            continue
        kv_bank_wait = [[t_mean], [t_msq], [t_ea], [t_eb]]
        last_mm_per_chunk = {}
        xn_last = None
        for c in range(NCH):
            sx = xTf_r[c % 4]
            t_x2 = plan.tdma(xTf[c % 4][:],
                             src[t0:t0 + TB, c * 128:(c + 1) * 128], waits=sx.take())
            xs = xnc_r[c % 2]
            if own:
                o = ((blk - NB) * NCH + c) * TB
                xdst = xnT_own[:, o:o + TB]
            else:
                xdst = xnc[c % 2][:]
            plan.op("ve", lambda h, i=c % 4: h.tensor_tensor(
                rope_r[:], xTf[i][:], ab_sb[:, 0:TB], ALU.mult),
                waits=[t_x2, t_ea, t_eb] + (xs.take() if not own else []))
            xn_last = plan.op("ve", lambda h, xd=xdst: h.tensor_tensor(
                xd, rope_r[:], ab_sb[:, TB:2 * TB], ALU.add))
            sx.frees.append(xn_last)
            ab_s.frees.append(xn_last)
            if own or P1_SUB < 4:
                continue
            mm = None
            for kv in range(NKV):
                mm = plan.op("pe", lambda h, c=c, kv=kv, i=c % 2: h.matmul(
                    pbank[kv][:],
                    wk_sb[:].rearrange("p (c n) -> p c n", n=512)[:, c, kv * 128:(kv + 1) * 128],
                    xnc[i][:], start=(c == 0), stop=(c == NCH - 1)),
                    waits=[xn_last] + ([t_wk] + kv_bank_wait[kv] + psfree[kv]
                                       if c == 0 else []))
            for ts in range(4):
                mm = plan.op("pe", lambda h, c=c, ts=ts, i=c % 2: h.matmul(
                    pbank[4 + ts][:], xnc[i][:, ts * 128:(ts + 1) * 128],
                    wv_sb[:].rearrange("p (c n) -> p c n", n=512)[:, c, :],
                    start=(c == 0), stop=(c == NCH - 1)),
                    waits=([t_wv] + psfree[4 + ts] if c == 0 else []))
            xs.frees.append(mm)
        if own or P1_SUB < 5:
            continue

        t_kT = None
        plan.op("ve", (lambda h: h.nop(cycle_cnt=DELAY)) if DELAY else
                (lambda h: h.tensor_copy(fen_v[:], ones1f[0:1, 0:4])), waits=[mm])
        for kv in range(NKV):
            kslot = kraw_r[kv % 2]
            t_kr = plan.op("ve", lambda h, kv=kv, i=kv % 2: h.tensor_scalar(
                kraw[i][:], pbank[kv][:], bk_c[:, kv:kv + 1], None, ALU.add),
                waits=[mm, t_bk] + kslot.take())
            psfree[kv] = [t_kr]
            plan.op("ve", lambda h, i=kv % 2: h.tensor_copy(
                rope_r[0:64, :], kraw[i][64:128, :]), waits=[t_kr])
            plan.op("ve", lambda h, i=kv % 2: h.tensor_copy(
                rope_r[64:128, :], kraw[i][0:64, :]))
            plan.op("ve", lambda h, i=kv % 2, t0=t0: h.tensor_tensor(
                rope_a[:], kraw[i][:], cosT[:, t0:t0 + TB], ALU.mult),
                waits=[t_cos])
            plan.op("ve", lambda h, t0=t0: h.tensor_tensor(
                rope_b[:], rope_r[:], sinS[:, t0:t0 + TB], ALU.mult),
                waits=[t_sin])
            t_kT = plan.op("ve", lambda h, kv=kv, t0=t0: h.tensor_tensor(
                kT[:, kv * T + t0:kv * T + t0 + TB], rope_a[:], rope_b[:], ALU.add))
            kslot.frees.append(t_kT)
        t_v = None
        last_pe_p1 = mm
        for ts in range(4):
            tt = blk * 4 + ts
            t_v = plan.op("ve", lambda h, tt=tt, ts=ts: h.tensor_copy(
                v_sb[:, tt * 512:(tt + 1) * 512], pbank[4 + ts][:]), waits=[mm])
            psfree[4 + ts] = [t_v]

    t_p1_ve = plan.op("ve", lambda h: h.tensor_copy(fen_v[:], ones1f[0:1, 0:4]),
                      waits=[last_pe_p1], sem="vC")
    t_p1_sc = plan.op("sc", lambda h: h.copy(fen_s[:], ones1f[0:1, 0:4]),
                      waits=[last_pe_p1], sem="sC")
    t_p1_pe = last_pe_p1
    G1 = [t_p1_pe, t_p1_ve, t_p1_sc]

    if debug >= 1:
        plan.dma("gp", dbg["dbg_xnT"], xnT_own[:], waits=G1)
        plan.dma("gp", dbg["dbg_kT"], kT[:], waits=G1)
        plan.dma("gp", dbg["dbg_v"], v_sb[:], waits=G1)
        plan.dma("gp", dbg["dbg_cos"], cosT[:], waits=G1)
        plan.dma("gp", dbg["dbg_sin"], sinS[:], waits=G1)
        dbg["dbg_bias"] = nc.dram_tensor("dbg_bias", [128, 24], F32,
                                         kind="ExternalOutput").ap()
        if debug >= 2:
            dbg["dbg_oT"] = nc.dram_tensor("dbg_oT", [128, NH * TB], BF16,
                                           kind="ExternalOutput").ap()
        plan.dma("gp", dbg["dbg_bias"][:, 0:16], bq_c[:], waits=G1)
        plan.dma("gp", dbg["dbg_bias"][:, 16:20], bk_c[:], waits=G1)
        plan.dma("gp", dbg["dbg_bias"][:, 20:24], bv_c[:], waits=G1)

    ctx1.close()
    plan.cuts = {e: len(plan.lists[e]) for e in ENGS}
    stop_after_p1 = (debug >= 9)

    meta = dict(nc=nc, ctx0=ctx0, ctx12=ctx12, plan=plan, dbg=dbg,
                stop_after_p1=stop_after_p1)
    if stop_after_p1:
        ctx12.close()
        return meta

    ctx2 = ExitStack()
    sb2 = mk_sb(ctx2)
    wq_t = [sb2(f"wq_t{i}", [128, D]) for i in range(2)]
    wo_t = [sb2(f"wo_t{i}", [128, D]) for i in range(4)]
    qT_t = [sb2(f"qT_t{i}", [128, TB]) for i in range(2)]
    pT_t = [sb2(f"pT_t{i}", [128, TB]) for i in range(4)]
    pexp = sb2("pexp", [128, TB])
    row_s = sb2("row_s", [1, TB], F32)
    bc_s = sb2("bc_s", [128, TB])
    ot1 = sb2("ot1", [128, TB])
    oT = sb2("oT", [128, NH * TB])
    xo_t = [sb2(f"xo_t{i}", [128, TB]) for i in range(2)]
    s_st = [sb2(f"s_st{i}", [128, TB]) for i in range(2)]

    p2_psfree0 = {i: list(psfree[i]) for i in range(8)}
    kraw_init = [list(s.frees) for s in kraw_r]
    sstore_tok = {}
    t_ss_last_box = [None]

    def plan_p2(v):
        plan.begin_variant(v)
        psf = {i: list(p2_psfree0[i]) for i in range(8)}
        wq_free = [list(G1), list(G1)]
        wo_r = ring(4, G1)
        qT_r = ring(2)
        pT_r = ring(4)
        pexp_s = Slot()
        xo_r = ring(2, G1)
        sst_r = ring(2, G1)
        kr_r = [Slot(kraw_init[0]), Slot(kraw_init[1])]
        oT_s = Slot()
        rows_s = Slot()
        bc_slot = Slot()
        ot1_s = Slot()
        q0s = Q0S[v]
        t_oT_last = None
        pend = [None]

        def tail_finish():
            if pend[0] is None:
                return None
            h_p, last_pv_p, t_rc_p, pv_b, rs_b = pend[0]
            pend[0] = None
            kv_p = h_p // 4
            t_bcm = plan.op("pe", lambda h, b=rs_b: h.matmul(
                pbank[b][:], ones1f[:], row_s[:], start=True, stop=True),
                waits=[t_rc_p] + psf[rs_b])
            rows_s.frees.append(t_bcm)
            plan.op("sc", (lambda h: h.nop(cycle_cnt=DELAY)) if DELAY else
                    (lambda h: h.copy(fen_s[:], ones1f[0:1, 0:4])),
                    waits=[t_bcm])
            t_bcc = plan.op("sc", lambda h, b=rs_b: h.copy(bc_s[:], pbank[b][:]),
                            waits=[t_bcm] + bc_slot.take())
            psf[rs_b] = [t_bcc]
            t_o1 = plan.op("ve", lambda h, b=pv_b: h.tensor_tensor(
                ot1[:], pbank[b][:], bc_s[:], ALU.mult),
                waits=[last_pv_p, t_bcc] + ot1_s.take())
            psf[pv_b] = [t_o1]
            bc_slot.frees.append(t_o1)
            t_oT = plan.op("ve", lambda h, h_i=h_p, kv=kv_p: h.tensor_scalar(
                oT[:, h_i * TB:(h_i + 1) * TB], ot1[:], bv_c[:, kv:kv + 1],
                None, ALU.add),
                waits=[t_o1] + (oT_s.take() if h_p == 0 else []))
            ot1_s.frees.append(t_oT)
            return t_oT

        wq_ld = [None, None]

        def wq_load(hh):
            i2 = hh % 2
            t = plan.dma("sp", wq_t[i2][:], wq_d[hh], waits=wq_free[i2])
            wq_free[i2] = []
            wq_ld[i2] = t
            return t

        def emit_wq(qb_w, hh):
            i2 = hh % 2
            qbank = 2 if i2 == 0 else 5
            t_wq = wq_ld[i2]
            mm = None
            for c in range(NCH):
                mm = plan.op("pe", lambda h, c=c, i=i2, qb=qb_w, b=qbank: h.matmul(
                    pbank[b][:], wq_t[i][:, c * 128:(c + 1) * 128],
                    xnT_own[:, (qb * NCH + c) * TB:(qb * NCH + c + 1) * TB],
                    start=(c == 0), stop=(c == NCH - 1)),
                    waits=[t_wq] + ([t_p1_ve] + psf[qbank] if c == 0 else []),
                    sem="pQ" if c == NCH - 1 else None)
            wq_free[i2] = [mm]
            return mm

        mm_pend = None
        for qb in range(NQB):
            q0 = q0s[qb]
            n_kt = (q0 + TB) // 128
            if qb == 0:
                wq_load(0)
                mm_pend = emit_wq(qb, 0)
                wq_load(1)
            for h_i in range(NH):
                mm = mm_pend
                qbank = 2 if h_i % 2 == 0 else 5
                k_fl = qb * NH + h_i
                if k_fl + 2 < NQB * NH:
                    wq_load((k_fl + 2) % NH)
                ic_conv = k_fl
                t_cg = plan.dma("gp", wgb_d[ic_conv], wg_d[ic_conv])
                t_cu = plan.dma("gp", wub_d[ic_conv], wu_d[ic_conv])
                t_cd = plan.dma("gp", wdb_d[ic_conv], wd_d[ic_conv])
                if v == 0:
                    conv_g[ic_conv] = t_cg
                    conv_u[ic_conv] = t_cu
                    conv_d_toks.append(t_cd)
                krs = kr_r[h_i % 2]
                plan.op("ve", (lambda h: h.nop(cycle_cnt=DELAY)) if DELAY else
                        (lambda h: h.tensor_copy(fen_v[:], ones1f[0:1, 0:4])),
                        waits=[mm])
                t_qr = plan.op("ve", lambda h, h_i=h_i, i=h_i % 2, b=qbank: h.tensor_scalar(
                    kraw[i][:], pbank[b][:], bq_c[:, h_i:h_i + 1], None, ALU.add),
                    waits=[mm] + krs.take())
                psf[qbank] = [t_qr]
                qts = qT_r[h_i % 2]
                qfr = qts.take()
                plan.op("ve", lambda h, i=h_i % 2: h.tensor_copy(
                    rope_r[0:64, :], kraw[i][64:128, :]), waits=[t_qr] + qfr)
                plan.op("ve", lambda h, i=h_i % 2: h.tensor_copy(
                    rope_r[64:128, :], kraw[i][0:64, :]))
                plan.op("ve", lambda h, i=h_i % 2, q0=q0: h.tensor_tensor(
                    rope_a[:], kraw[i][:], cosT[:, q0:q0 + TB], ALU.mult))
                plan.op("ve", lambda h, q0=q0: h.tensor_tensor(
                    rope_b[:], rope_r[:], sinS[:, q0:q0 + TB], ALU.mult))
                t_q = plan.op("ve", lambda h, i=h_i % 2: h.tensor_tensor(
                    qT_t[i][:], rope_a[:], rope_b[:], ALU.add))
                krs.frees.append(t_q)
                if h_i + 1 < NH:
                    mm_pend = emit_wq(qb, h_i + 1)
                t_pt = tail_finish()
                if t_pt is not None:
                    t_oT_last = t_pt

                kv = h_i // 4
                last_pv = last_row = None
                t_ps = [None] * n_kt
                for kt in range(n_kt + 1):
                    if kt < n_kt:
                        sbk = kt % 2
                        if kt == 0:
                            plan.op("pe", lambda h: h.nop(cycle_cnt=DELAY),
                                    waits=[t_q])
                        t_s = plan.op("pe", lambda h, kt=kt, kv=kv, i=h_i % 2, sbk=sbk: h.matmul(
                            pbank[sbk][:], kT[:, kv * T + kt * 128:kv * T + (kt + 1) * 128],
                            qT_t[i][:], start=True, stop=True),
                            waits=[t_q] + psf[sbk])
                        pslot = pT_r[kt % 4]
                        pfr = pslot.take()
                        if kt >= n_kt - 4:
                            t_e = plan.op("sc", lambda h, sbk=sbk: h.activation(
                                pexp[:], pbank[sbk][:], AF.Exp, scale=SCALE),
                                waits=[t_s] + pexp_s.take())
                            dd = kt - (n_kt - 4)
                            t_p = plan.op("ve", lambda h, kt=kt, dd=dd: h.tensor_tensor(
                                pT_t[kt % 4][:], pexp[:], m0[:, 384 - 128 * dd:896 - 128 * dd],
                                ALU.mult), waits=[t_e, t_m0] + pfr)
                            pexp_s.frees.append(t_p)
                        else:
                            t_p = plan.op("sc", lambda h, kt=kt, sbk=sbk: h.activation(
                                pT_t[kt % 4][:], pbank[sbk][:], AF.Exp, scale=SCALE),
                                waits=[t_s] + pfr)
                        psf[sbk] = [t_p]
                        t_ps[kt] = t_p
                    if kt > 0:
                        ku = kt - 1
                        rs_b = 3 if h_i % 2 == 0 else 7
                        pv_b = 4 if h_i % 2 == 0 else 6
                        last_row = plan.op("pe", lambda h, ku=ku, n=n_kt, b=rs_b: h.matmul(
                            pbank[b][0:1, :], ones_bf[:], pT_t[ku % 4][:],
                            start=(ku == 0), stop=(ku == n - 1)),
                            waits=[t_ps[ku]] + (psf[rs_b] if ku == 0 else []))
                        last_pv = plan.op("pe", lambda h, ku=ku, kv=kv, n=n_kt, b=pv_b: h.matmul(
                            pbank[b][:], v_sb[:, ku * 512 + kv * 128:ku * 512 + (kv + 1) * 128],
                            pT_t[ku % 4][:], start=(ku == 0), stop=(ku == n - 1)),
                            waits=(psf[pv_b] if ku == 0 else []))
                        pT_r[ku % 4].frees.append(last_pv)
                qts.frees.append(last_pv)
                rs_b = 3 if h_i % 2 == 0 else 7
                plan.op("ve", (lambda h: h.nop(cycle_cnt=DELAY)) if DELAY else
                        (lambda h: h.tensor_copy(fen_v[:], ones1f[0:1, 0:4])),
                        waits=[last_row])
                t_rc = plan.op("ve", lambda h, b=rs_b: h.reciprocal(
                    row_s[:], pbank[b][0:1, :]),
                    waits=[last_row] + rows_s.take())
                psf[rs_b] = [t_rc]
                pend[0] = (h_i, last_pv, t_rc, 4 if h_i % 2 == 0 else 6, rs_b)
            if qb + 1 < NQB:
                mm_pend = emit_wq(qb + 1, 0)
            t_pt = tail_finish()
            if t_pt is not None:
                t_oT_last = t_pt

            plan.op("pe", lambda h: h.nop(cycle_cnt=DELAY), waits=[t_oT_last])
            for db in range(4):
                g_toks = [None] * 4
                for g in range(4):
                    g_toks[g] = plan.dma("sp", wo_t[g][:], wo_d[db, g],
                                         waits=wo_r[g].take())
                for ts in range(4):
                    xos = xo_r[(db * 4 + ts) % 2]
                    t_xo = plan.dma("sp", xo_t[(db * 4 + ts) % 2][:],
                                    x_own[qb * TB + ts * 128:qb * TB + (ts + 1) * 128,
                                          db * 512:(db + 1) * 512],
                                    waits=xos.take())
                    bank = 6 + (db * 4 + ts) % 2
                    xi4 = (db * 4 + ts) % 2
                    mm = None
                    for h_i in range(NH):
                        g = h_i // 4
                        w = [t_oT_last] + (psf[bank] if h_i == 0 else [])
                        if ts == 0 and h_i % 4 == 0:
                            w.append(g_toks[g])
                        mm = plan.op("pe", lambda h, h_i=h_i, ts=ts, g=g, bank=bank: h.matmul(
                            pbank[bank][:],
                            oT[:, h_i * TB + ts * 128:h_i * TB + (ts + 1) * 128],
                            wo_t[g][:, (h_i % 4) * 512:(h_i % 4 + 1) * 512],
                            start=(h_i == 0), stop=(h_i == NH - 1)),
                            waits=w, sem="pW" if h_i % 4 == 3 else None)
                        if ts == 3 and h_i % 4 == 3:
                            wo_r[g].frees.append(mm)
                    oT_s.frees.append(mm)
                    si = (db * 4 + ts) % 2
                    sst = sst_r[si]
                    plan.op("ve", (lambda h: h.nop(cycle_cnt=DELAY)) if DELAY else
                            (lambda h: h.tensor_copy(fen_v[:], ones1f[0:1, 0:4])),
                            waits=[mm], sem="vS")
                    t_sv = plan.op("ve", lambda h, i=si, xi=si, bank=bank: h.tensor_tensor(
                        s_st[i][:], pbank[bank][:], xo_t[xi][:], ALU.add),
                        waits=[mm, t_xo] + sst.take(), sem="vS")
                    psf[bank] = [t_sv]
                    xos.frees.append(t_sv)
                    t_sve = plan.op("ve", lambda h: h.tensor_copy(
                        fen_v[:], ones1f[0:1, 0:4]), sem="vS")
                    t_ss = plan.dma("gp", s_scr[qb * TB + ts * 128:qb * TB + (ts + 1) * 128,
                                               db * 512:(db + 1) * 512],
                                    s_st[si][:], waits=[t_sve])
                    if debug >= 2:
                        t_ss = plan.dma("gp",
                                        dbg["dbg_s"][qb * TB + ts * 128:qb * TB + (ts + 1) * 128,
                                                     db * 512:(db + 1) * 512],
                                        s_st[si][:], waits=[t_sve])
                    sst.frees.append(t_ss)
                    sstore_tok[(qb, ts, db)] = t_ss
                    t_ss_last_box[0] = t_ss

        t_end_pe = mm
        t_end_ve = plan.op("ve", lambda h: h.tensor_copy(
            fen_v[:], ones1f[0:1, 0:4]), waits=[t_end_pe], sem="vS")
        t_end_sc = plan.op("sc", lambda h: h.copy(
            fen_s[:], ones1f[0:1, 0:4]), waits=[t_end_pe], sem="sE")
        for i in range(8):
            psf[i] = [t_end_pe, t_end_ve, t_end_sc]
        plan.end_variant()
        return psf, [t_end_pe, t_end_ve, t_end_sc]

    psf_a, G2 = plan_p2(0)
    plan_p2(1)
    if debug >= 2:
        plan.dma("gp", dbg["dbg_oT"], oT[:], waits=G2[:3])
    psfree = psf_a
    G2 = G2 + [t_ss_last_box[0]]
    ctx2.close()
    ctx12.close()
    if not P3_ON:
        ctx0 if True else None
        meta["skip_p3"] = True
        return meta

    ctx3 = ExitStack()
    sb3 = mk_sb(ctx3)
    wg_t = [sb3(f"wg_t{i}", [128, D]) for i in range(3)]
    wu_t = [sb3(f"wu_t{i}", [128, D]) for i in range(3)]
    gs_t = [sb3(f"gs_t{i}", [128, 512], F32) for i in range(2)]
    h_g = sb3("h_g", [128, 4 * 512], F32)
    a_g = sb3("a_g", [128, 4 * 512], F32)
    b_g = sb3("b_g", [128, 6 * 512])
    w_g = sb3("w_g", [128, 2 * 512])
    m_g = sb3("m_g", [128, 4 * 512])
    hT_slab = sb3("hT_slab", [128, NIC * TB], side="right")
    wd_t = [sb3(f"wd_t{i}", [128, 1024]) for i in range(3)]
    sld = [sb3(f"sld{i}", [128, 512]) for i in range(2)]
    y_st = [sb3(f"y_st{i}", [128, 512], F32) for i in range(2)]

    wg_r = ring(3, G2)
    wu_r = ring(3, G2)
    gs_r = ring(2, G2)
    wd_r = ring(3, G2)
    sld_r = ring(2, G2)
    yst_r = ring(2, G2)
    slab_free = list(G2)
    for ob in range(NOB):
        grp_toks = []
        for ic in range(NIC):
            ii = ic % 2
            iw = ic % 3
            t_wg = plan.dma("gp", wg_t[iw][:], wgb_d[ic],
                            waits=wg_r[iw].take() + ([conv_g[ic]] if ob == 0 else []))
            t_wu = plan.dma("gp", wu_t[iw][:], wub_d[ic],
                            waits=wu_r[iw].take() + ([conv_u[ic]] if ob == 0 else []))
            bank_g = ic % 2
            bank_u = 2 + ic % 2
            mmg = mmu = None
            for c in range(NCH):
                mmg = plan.op("pe", lambda h, c=c, iw=iw, ob=ob, bank=bank_g: h.matmul(
                    pbank[bank][:], wg_t[iw][:, c * 128:(c + 1) * 128],
                    xnT_own[:, (ob * NCH + c) * TB:(ob * NCH + c + 1) * TB],
                    start=(c == 0), stop=(c == NCH - 1)),
                    waits=[t_wg] + (psfree[bank_g] if c == 0 else []))
            wg_r[iw].frees.append(mmg)
            for c in range(NCH):
                mmu = plan.op("pe", lambda h, c=c, iw=iw, ob=ob, bank=bank_u: h.matmul(
                    pbank[bank][:], wu_t[iw][:, c * 128:(c + 1) * 128],
                    xnT_own[:, (ob * NCH + c) * TB:(ob * NCH + c + 1) * TB],
                    start=(c == 0), stop=(c == NCH - 1)),
                    waits=[t_wu] + (psfree[bank_u] if c == 0 else []))
            wu_r[iw].frees.append(mmu)
            gss = gs_r[ii]
            plan.op("sc", (lambda h: h.nop(cycle_cnt=DELAY_S)) if DELAY else
                    (lambda h: h.copy(fen_s[:], ones1f[0:1, 0:4])),
                    waits=[mmg])
            t_gs = plan.op("sc", lambda h, ii=ii, bank=bank_g: h.activation(
                gs_t[ii][:], pbank[bank][:], AF.Silu), waits=[mmg] + gss.take())
            psfree[bank_g] = [t_gs]
            plan.op("ve", (lambda h: h.nop(cycle_cnt=DELAY_S)) if DELAY else
                    (lambda h: h.tensor_copy(fen_v[:], ones1f[0:1, 0:4])),
                    waits=[mmu])
            t_h = plan.op("ve", lambda h, ic=ic, ii=ii, bank=bank_u: h.tensor_tensor(
                h_g[:, (ic % 4) * 512:(ic % 4 + 1) * 512], gs_t[ii][:],
                pbank[bank][:], ALU.mult), waits=[mmu, t_gs])
            psfree[bank_u] = [t_h]
            gss.frees.append(t_h)

            if ic % 4 == 3:
                t_abs = None
                for j in range(4):
                    t_abs = plan.op("sc", lambda h, j=j: h.activation(
                        a_g[:, j * 512:(j + 1) * 512], h_g[:, j * 512:(j + 1) * 512],
                        AF.Abs),
                        waits=[t_h] + (slab_free if (j == 0 and ic == 3) else []))
                prs = [(0, 1), (0, 2), (0, 3), (1, 2), (1, 3), (2, 3)]
                for k, (i1, i2) in enumerate(prs):
                    plan.op("ve", lambda h, k=k, i1=i1, i2=i2: h.tensor_tensor(
                        b_g[:, k * 512:(k + 1) * 512], a_g[:, i1 * 512:(i1 + 1) * 512],
                        a_g[:, i2 * 512:(i2 + 1) * 512], ALU.is_ge),
                        waits=([t_abs] if k == 0 else []))
                Bx = lambda k: b_g[:, k * 512:(k + 1) * 512]
                W0 = w_g[:, 0:512]
                W1 = w_g[:, 512:1024]
                Mx = lambda j: m_g[:, j * 512:(j + 1) * 512]
                plan.op("ve", lambda h: h.tensor_tensor(W0, Bx(0), Bx(1), ALU.add))
                plan.op("ve", lambda h: h.tensor_tensor(W0, W0, Bx(2), ALU.add))
                plan.op("ve", lambda h: h.tensor_single_scalar(Mx(0), W0, 1.5, ALU.is_ge))
                plan.op("ve", lambda h: h.tensor_tensor(W0, Bx(3), Bx(4), ALU.add))
                plan.op("ve", lambda h: h.tensor_tensor(W0, W0, Bx(0), ALU.subtract))
                plan.op("ve", lambda h: h.tensor_single_scalar(Mx(1), W0, 0.5, ALU.is_ge))
                plan.op("ve", lambda h: h.tensor_tensor(W0, Bx(1), Bx(3), ALU.add))
                plan.op("ve", lambda h: h.tensor_tensor(W0, Bx(5), W0, ALU.subtract))
                plan.op("ve", lambda h: h.tensor_single_scalar(Mx(2), W0, -0.5, ALU.is_ge))
                plan.op("ve", lambda h: h.tensor_tensor(W1, Bx(2), Bx(4), ALU.add))
                plan.op("ve", lambda h: h.tensor_tensor(W1, W1, Bx(5), ALU.add))
                plan.op("ve", lambda h: h.tensor_single_scalar(Mx(3), W1, 1.5, ALU.is_le))
                tk = None
                for j in range(4):
                    icj = ic - 3 + j
                    tk = plan.op("ve", lambda h, j=j, icj=icj: h.tensor_tensor(
                        hT_slab[:, icj * TB:(icj + 1) * TB],
                        h_g[:, j * 512:(j + 1) * 512], Mx(j), ALU.mult))
                grp_toks.append(tk)

        for dbp in range(2):
            last_mm = [None] * 8
            for ic in range(NIC):
                iw = ic % 3
                t_wd = plan.dma("gp", wd_t[iw][:],
                                wdb_d[ic][:, dbp * 1024:(dbp + 1) * 1024],
                                waits=wd_r[iw].take() +
                                ([conv_d_toks[ic]] if ob == 0 and dbp == 0 else []))
                gtok = grp_toks[ic // 4]
                for db2 in range(2):
                    for ts in range(4):
                        bank = ts + 4 * db2
                        w = [t_wd, gtok]
                        if ic == 0:
                            w = w + psfree[bank]
                        last_mm[bank] = plan.op(
                            "pe",
                            lambda h, ic=ic, iw=iw, ts=ts, db2=db2, bank=bank: h.matmul(
                                pbank[bank][:],
                                hT_slab[:, ic * TB + ts * 128:ic * TB + (ts + 1) * 128],
                                wd_t[iw][:, db2 * 512:(db2 + 1) * 512],
                                start=(ic == 0), stop=(ic == NIC - 1)),
                            waits=w, sem="pD" if ic == NIC - 1 else None)
                wd_r[iw].frees.append(last_mm[7])
            for db2 in range(2):
                for ts in range(4):
                    bank = ts + 4 * db2
                    db = dbp * 2 + db2
                    si = (db2 * 4 + ts) % 2
                    t_sl = plan.dma("sp", sld[si][:],
                                    s_scr[ob * TB + ts * 128:ob * TB + (ts + 1) * 128,
                                          db * 512:(db + 1) * 512],
                                    waits=sld_r[si].take() + [sstore_tok[(ob, ts, db)]])
                    plan.op("ve", (lambda h: h.nop(cycle_cnt=DELAY_S)) if DELAY
                            else (lambda h: h.tensor_copy(fen_v[:], ones1f[0:1, 0:4])),
                            waits=[last_mm[bank]])
                    t_y = plan.op("ve", lambda h, i=si, bank=bank: h.tensor_tensor(
                        y_st[i][:], pbank[bank][:], sld[i][:], ALU.add),
                        waits=[last_mm[bank], t_sl] + yst_r[si].take())
                    psfree[bank] = [t_y]
                    sld_r[si].frees.append(t_y)
                    t_yd = plan.dma("gp", y_d[ob * TB + ts * 128:ob * TB + (ts + 1) * 128,
                                              db * 512:(db + 1) * 512],
                                    y_st[si][:], waits=[t_y])
                    yst_r[si].frees.append(t_yd)
        slab_free = [last_mm[7]]
    ctx3.close()
    return meta


def finish(meta):
    nc, plan = meta["nc"], meta["plan"]
    plan.resolve()
    with meta["ctx0"]:
        with ExitStack() as sctx:
            sems = {name: sctx.enter_context(nc.semaphore(name)) for name in ALLSEMS}
            lastw = {}

            def emit_ops(h, eng, ops):
                for waits, emit, tok, dma in ops:
                    need = {}
                    for w in waits:
                        if (w.eng == eng and eng != "gp" and not dma
                                and w.sem not in DMASEMS):
                            continue
                        assert w.sem is not None and w.val is not None, \
                            ("unresolved cross wait", eng, w.eng, w.key)
                        need[w.sem] = max(need.get(w.sem, 0), w.val)
                    for sname in sorted(need):
                        val = need[sname]
                        assert val <= plan.final[sname], \
                            ("wait exceeds final -> hang", eng, sname, val,
                             plan.final[sname])
                        key = (eng, sname)
                        if lastw.get(key, 0) >= val:
                            continue
                        h.wait_ge(sems[sname], val)
                        lastw[key] = val
                    inst = emit(h)
                    if tok.sem is not None and tok.val is not None:
                        if eng == "gp" and not dma:
                            inst = h.drain()
                        inst.then_inc(sems[tok.sem], 16 if dma else 1)

            cuts = plan.cuts
            with nc.Block() as b1:
                @b1.sync
                def _(h):
                    emit_ops(h, "sp", plan.lists["sp"][:cuts["sp"]])

                @b1.gpsimd
                def _(h):
                    emit_ops(h, "gp", plan.lists["gp"][:cuts["gp"]])

                @b1.tensor
                def _(h):
                    emit_ops(h, "pe", plan.lists["pe"][:cuts["pe"]])

                @b1.vector
                def _(h):
                    emit_ops(h, "ve", plan.lists["ve"][:cuts["ve"]])

                @b1.scalar
                def _(h):
                    emit_ops(h, "sc", plan.lists["sc"][:cuts["sc"]])

            if not meta["stop_after_p1"]:
                pidx = [nc.tensor.partition_id() & 1,
                        nc.vector.partition_id() & 1,
                        nc.scalar.partition_id() & 1]
                saved = dict(lastw)
                for v in nc.Switch(engines=[nc.tensor, nc.vector, nc.scalar],
                                   index=pidx, n=2):
                    lastw.clear()
                    lastw.update(saved)
                    emit_ops(nc.tensor, "pe", plan.var[v]["pe"])
                    emit_ops(nc.vector, "ve", plan.var[v]["ve"])
                    emit_ops(nc.scalar, "sc", plan.var[v]["sc"])
                lastw.clear()
                lastw.update(saved)

            with nc.Block() as b2:
                @b2.sync
                def _(h):
                    emit_ops(h, "sp", plan.lists["sp"][cuts["sp"]:])

                @b2.tensor
                def _(h):
                    emit_ops(h, "pe", plan.lists["pe"][cuts["pe"]:])

                @b2.vector
                def _(h):
                    emit_ops(h, "ve", plan.lists["ve"][cuts["ve"]:])

                @b2.scalar
                def _(h):
                    emit_ops(h, "sc", plan.lists["sc"][cuts["sc"]:])

                @b2.gpsimd
                def _(h):
                    emit_ops(h, "gp", plan.lists["gp"][cuts["gp"]:])
                    h.wait_ge(sems["gL"], plan.final["gL"])
    return nc


def ffn_perm():
    n = np.arange(I_DIM)
    return 4 * ((n // 512) * 128 + (n % 128)) + (n // 128) % 4


def make_in_maps(inp):
    x = np.asarray(inp["L0_attn_x"])
    Wq = np.asarray(inp["L0_attn_Wq"])
    Wk = np.asarray(inp["L0_attn_Wk"])
    Wv = np.asarray(inp["L0_attn_Wv"])
    Wo = np.asarray(inp["L0_attn_Wo"])
    bq = np.ascontiguousarray(np.asarray(inp["L0_attn_bq"]).reshape(1, D))
    bk = np.ascontiguousarray(np.asarray(inp["L0_attn_bk"]).reshape(1, 512))
    bv = np.ascontiguousarray(np.asarray(inp["L0_attn_bv"]).reshape(1, 512))
    perm = ffn_perm()
    Wg = np.asarray(inp["L0_ffn_w_gate"])[:, perm]
    Wu = np.asarray(inp["L0_ffn_w_up"])[:, perm]
    Wd = np.asarray(inp["L0_ffn_w_down"])[perm, :]

    wq_h = np.ascontiguousarray(
        Wq.reshape(NCH, 128, NH, 128).transpose(2, 1, 0, 3).reshape(NH, 128, D))
    wk_h = np.ascontiguousarray(Wk.reshape(NCH, 128, 512).transpose(1, 0, 2))
    wv_h = np.ascontiguousarray(Wv.reshape(NCH, 128, 512).transpose(1, 0, 2))
    wo_h = np.ascontiguousarray(
        Wo.reshape(4, 4, 128, 4, 512).transpose(3, 0, 2, 1, 4).reshape(4, 4, 128, D))
    wg_h = np.ascontiguousarray(
        Wg.reshape(NCH, 128, NIC, 128).transpose(2, 1, 0, 3).reshape(NIC, 128, D))
    wu_h = np.ascontiguousarray(
        Wu.reshape(NCH, 128, NIC, 128).transpose(2, 1, 0, 3).reshape(NIC, 128, D))
    wd_h = np.ascontiguousarray(Wd.reshape(NIC, 128, D))

    in_maps = []
    for c in range(8):
        b, half = c // 2, c % 2
        own = Q0S[half]
        in_maps.append(dict(
            x_seq=np.ascontiguousarray(x[b]),
            x_own=np.ascontiguousarray(
                np.concatenate([x[b, s:s + TB] for s in own], axis=0)),
            wq=wq_h, wk=wk_h, wv=wv_h, wo=wo_h, bq=bq, bk=bk, bv=bv,
            wg=wg_h, wu=wu_h, wd=wd_h))
    return in_maps


def assemble(results):
    out = np.zeros((4, T, D), dtype=np.float32)
    for c in range(8):
        b, half = c // 2, c % 2
        y = np.asarray(results[c]["y"])
        for i, s in enumerate(Q0S[half]):
            out[b, s:s + TB] = y[i * TB:(i + 1) * TB]
    return out


_CACHE = {}


def kernel(**inputs):
    from concourse.bass_utils import run_bass_kernel_spmd
    in_maps = make_in_maps(inputs)
    if "nc" not in _CACHE:
        meta = build(debug=0)
        _CACHE["nc"] = finish(meta)
    res = run_bass_kernel_spmd(_CACHE["nc"], in_maps, core_ids=list(range(8)))
    return assemble(res.results)
